# version 06
# baseline (speedup 1.0000x reference)
import math
import numpy as np
import ml_dtypes
from contextlib import ExitStack

import concourse.bass as bass
import concourse.mybir as mybir
from concourse.bass_utils import run_bass_kernel_spmd

F32 = mybir.dt.float32
BF16 = mybir.dt.bfloat16
I32 = mybir.dt.int32
ALU = mybir.AluOpType
ACT = mybir.ActivationFunctionType
AX = mybir.AxisListType
BF16NP = ml_dtypes.bfloat16

D = 2048
H = 16
DK = 128
TB = 4096
TOWN = 2048
IREAL = 6464
IP = 6656
NCD = 16
NCI = 52
NGA = 8
NGO = 4
RSQDK = 1.0 / math.sqrt(DK)
CLIP = 1.4999
NUMEL = [D * D] * 4 + [IREAL * D] * 2 + [D * IREAL]


class Ctr:

    def __init__(self, sem):
        self.sem = sem
        self.n = 0

    def inc(self, inst, k=1):
        inst.then_inc(self.sem, k)
        self.n += k
        return self.n



def build_main():
    nc = bass.Bass("TRN2", target_bir_lowering=False)
    dram = lambda n, s, dt, k: nc.dram_tensor(n, s, dt, kind=k).ap()

    x_in = dram("x", [TB, D], F32, "ExternalInput")
    fwq = dram("fwq", [16, 128, NCD * 128], F32, "ExternalInput")
    fwk = dram("fwk", [16, 128, NCD * 128], F32, "ExternalInput")
    fwv = dram("fwv", [4, 128, NCD * 512], F32, "ExternalInput")
    fwo = dram("fwo", [4, 128, NCD * 512], F32, "ExternalInput")
    fwg = dram("fwg", [13, 128, NCD * 512], F32, "ExternalInput")
    fwu = dram("fwu", [13, 128, NCD * 512], F32, "ExternalInput")
    fwd = dram("fwd", [NCI, 128, 2048], F32, "ExternalInput")
    twq = dram("twq", [16, 128, NCD * 128], BF16, "Internal")
    twk = dram("twk", [16, 128, NCD * 128], BF16, "Internal")
    twv = dram("twv", [4, 128, NCD * 512], BF16, "Internal")
    two = dram("two", [4, 128, NCD * 512], BF16, "Internal")
    twg = dram("twg", [13, 128, NCD * 512], BF16, "Internal")
    twu = dram("twu", [13, 128, NCD * 512], BF16, "Internal")
    twd = dram("twd", [NCI, 128, 2048], BF16, "Internal")
    ninvr = dram("ninvr", [1, 8], F32, "ExternalInput")
    g_in = {p: dram("g" + p, [D], F32, "ExternalInput") for p in "qkvo"}
    eyeb = dram("eyeb", [128, 128], BF16, "ExternalInput")
    eyef = dram("eyef", [128, 128], F32, "ExternalInput")
    onescf = dram("onescf", [128, 1], F32, "ExternalInput")
    onesrf = dram("onesrf", [1, 128], F32, "ExternalInput")
    ehs8 = dram("ehs8", [128, 8 * 128], BF16, "ExternalInput")
    out = dram("out", [TOWN, D], F32, "ExternalOutput")

    x1scr = dram("x1scr", [TOWN, D], F32, "Internal")
    qscr = dram("qscr", [NGO, H, 128, 512], BF16, "Internal")
    kscr = dram("kscr", [H, 128, TB], BF16, "Internal")
    vscr = dram("vscr", [H, 128, 32, 128], BF16, "Internal")

    ctx = ExitStack()
    E = ctx.enter_context
    with ctx:
        gbo = E(nc.sbuf_tensor("gbo", [128, D], BF16))
        cwb = E(nc.sbuf_tensor("cwb", [128, 8], F32))
        seyeb = E(nc.sbuf_tensor("seyeb", [128, 128], BF16))
        seyef = E(nc.sbuf_tensor("seyef", [128, 128], F32))
        sonescf = E(nc.sbuf_tensor("sonescf", [128, 1], F32))
        sonesrf = E(nc.sbuf_tensor("sonesrf", [1, 128], F32))
        sehs = E(nc.sbuf_tensor("sehs", [128, 8 * 128], BF16))
        sconst = E(nc.sbuf_tensor("sconst", [128, 4], F32))
        km2row = E(nc.sbuf_tensor("km2row", [1, 16], F32))
        srow8 = E(nc.sbuf_tensor("srow8", [1, 8], F32))
        sninvr = E(nc.sbuf_tensor("sninvr", [1, 8], F32))
        swrow = E(nc.sbuf_tensor("swrow", [1, 8], F32))
        norm32 = E(nc.sbuf_tensor("norm32", [32, 32], F32))
        normT = E(nc.sbuf_tensor("normT", [32, 32], F32))
        qrow = E(nc.sbuf_tensor("qrow", [1, 16], F32))
        xn = E(nc.sbuf_tensor("xn", [128, D], F32))
        ybuf = E(nc.sbuf_tensor("ybuf", [128, D], F32))
        rnd32 = E(nc.sbuf_tensor("rnd32", [128, D], F32))
        xqtm = E(nc.sbuf_tensor("xqtm", [128, D], BF16))
        bnst = E(nc.sbuf_tensor("bnst", [128, 24], F32))
        mv = E(nc.sbuf_tensor("mv", [128, 2], F32))
        tt8 = E(nc.sbuf_tensor("tt8", [128, 8], F32))
        alj = E(nc.sbuf_tensor("alj", [128, 12], F32))

        psA = E(nc.psum_tensor("psA", [128, 512], F32))
        psB = E(nc.psum_tensor("psB", [128, 512], F32))
        psC = E(nc.psum_tensor("psC", [128, 512], F32))
        psD = E(nc.psum_tensor("psD", [128, 512], F32))
        psE = E(nc.psum_tensor("psE", [128, 512], F32))
        psF = E(nc.psum_tensor("psF", [128, 512], F32))

        ld = Ctr(E(nc.semaphore("ld")))
        gd = Ctr(E(nc.semaphore("gd")))
        te = Ctr(E(nc.semaphore("te")))
        ve = Ctr(E(nc.semaphore("ve")))
        se = Ctr(E(nc.semaphore("se")))
        blk = E(nc.Block())

        def barrier():
            cs = [ld, gd, te, ve, se]
            vals = [c.n for c in cs]
            for emit, own in [(blk.sync, ld), (blk.tensor, te), (blk.vector, ve),
                              (blk.scalar, se), (blk.gpsimd, gd)]:
                def f(e, own=own, vals=tuple(vals)):
                    for c, v in zip(cs, vals):
                        if c is not own:
                            e.wait_ge(c.sem, v)
                emit(f)

        def emit_ln(src):
            @blk.vector
            def _(e):
                e.wait_ge(se.sem, se.n)
                for q in range(4):
                    ins = e.bn_stats(bnst[:, q * 6:q * 6 + 6],
                                     src[:, q * 512:q * 512 + 512])
                ve.inc(ins)
                e.wait_ge(ve.sem, ve.n)
                ve.inc(e.bn_aggr(mv[:], bnst[:]))

            @blk.scalar
            def _(e):
                e.wait_ge(ve.sem, ve.n)
                e.activation(tt8[:, 0:1], mv[:, 1:2], ACT.Ln, bias=sconst[:, 0:1])
                se.inc(e.activation(tt8[:, 0:1], tt8[:, 0:1], ACT.Exp, scale=-0.5))

            @blk.vector
            def _(e):
                e.wait_ge(se.sem, se.n)
                e.scalar_tensor_tensor(tt8[:, 1:2], mv[:, 0:1], -1.0,
                                       tt8[:, 0:1], ALU.mult, ALU.mult)
                ve.inc(e.tensor_tensor(tt8[:, 3:4], tt8[:, 0:1], tt8[:, 0:1],
                                       ALU.mult))
                e.wait_ge(ve.sem, ve.n)
                ve.inc(e.tensor_tensor(tt8[:, 3:4], mv[:, 1:2], tt8[:, 3:4],
                                       ALU.mult))

            @blk.scalar
            def _(e):
                e.wait_ge(ve.sem, ve.n)
                e.activation(xn[:], src[:], ACT.Identity, bias=tt8[:, 1:2],
                             scale=tt8[:, 0:1])
                e.activation(tt8[:, 2:3], tt8[:, 3:4], ACT.Ln, bias=sconst[:, 1:2])
                se.inc(e.activation(tt8[:, 2:3], tt8[:, 2:3], ACT.Exp, scale=-0.5))

        def emit_rms(src):
            @blk.vector
            def _(e):
                e.wait_ge(se.sem, se.n)
                for q in range(4):
                    ins = e.bn_stats(bnst[:, q * 6:q * 6 + 6],
                                     src[:, q * 512:q * 512 + 512])
                ve.inc(ins)
                e.wait_ge(ve.sem, ve.n)
                ve.inc(e.bn_aggr(mv[:], bnst[:]))
                e.wait_ge(ve.sem, ve.n)
                ve.inc(e.tensor_tensor(tt8[:, 3:4], mv[:, 0:1], mv[:, 0:1],
                                       ALU.mult))
                e.wait_ge(ve.sem, ve.n)
                ve.inc(e.tensor_tensor(tt8[:, 3:4], mv[:, 1:2], tt8[:, 3:4],
                                       ALU.add))

            @blk.scalar
            def _(e):
                e.wait_ge(ve.sem, ve.n)
                e.activation(tt8[:, 2:3], tt8[:, 3:4], ACT.Ln, bias=sconst[:, 1:2])
                se.inc(e.activation(tt8[:, 2:3], tt8[:, 2:3], ACT.Exp, scale=-0.5))

        def emit_quant(src, w, gain, use_rms, wcol, acol, extra_acols=()):
            yy = src if gain is None else ybuf

            @blk.vector
            def _(e):
                e.wait_ge(se.sem, se.n)
                e.wait_ge(te.sem, te.n)
                if gain is not None:
                    e.tensor_tensor(ybuf[:, 0:w], src[:, 0:w], gain[:, 0:w],
                                    ALU.mult)
                ve.inc(e.reduce_max(tt8[:, 4:5], yy[:, 0:w], AX.X,
                                    apply_absolute_value=True))
                e.wait_ge(ve.sem, ve.n)
                if use_rms:
                    ve.inc(e.tensor_tensor(tt8[:, 4:5], tt8[:, 4:5],
                                           tt8[:, 2:3], ALU.mult))
                    e.wait_ge(ve.sem, ve.n)
                ve.inc(e.tensor_scalar(tt8[:, 4:5], tt8[:, 4:5], 1e-5, None,
                                       ALU.max))

            @blk.scalar
            def _(e):
                e.wait_ge(ve.sem, ve.n)
                e.activation(tt8[:, 5:6], tt8[:, 4:5], ACT.Ln)
                se.inc(e.activation(tt8[:, 5:6], tt8[:, 5:6], ACT.Exp,
                                    scale=-1.0))

            @blk.vector
            def _(e):
                e.wait_ge(se.sem, se.n)
                if use_rms:
                    ve.inc(e.tensor_tensor(tt8[:, 6:7], tt8[:, 5:6],
                                           tt8[:, 2:3], ALU.mult))
                else:
                    ve.inc(e.tensor_copy(tt8[:, 6:7], tt8[:, 5:6]))
                e.wait_ge(ve.sem, ve.n)
                nch = (w + D - 1) // D
                for q in range(nch):
                    lo, hi = q * D, min(w, (q + 1) * D)
                    e.tensor_scalar(rnd32[:, 0:hi - lo].bitcast(I32),
                                    yy[:, lo:hi], tt8[:, 6:7], 127.0,
                                    ALU.mult, ALU.mult)
                    e.tensor_copy(xqtm[:, lo:hi], rnd32[:, 0:hi - lo].bitcast(I32))
                last = e.tensor_tensor(alj[:, acol:acol + 1], tt8[:, 4:5],
                                       cwb[:, wcol:wcol + 1], ALU.mult)
                for (wc2, ac2) in extra_acols:
                    last = e.tensor_tensor(alj[:, ac2:ac2 + 1], tt8[:, 4:5],
                                           cwb[:, wc2:wc2 + 1], ALU.mult)
                ve.inc(last)

        def emit_transposes(tbs, src_tm, ncc, dstap_fn):
            nb = (ncc + 7) // 8
            for b in range(nb):
                hi = min(8, ncc - b * 8)

                @blk.tensor
                def _(e, b=b, hi=hi):
                    e.wait_ge(ve.sem, ve.n)
                    for q in range(hi):
                        cc = b * 8 + q
                        te.inc(e.transpose(
                            tbs[b % 2][:, q * 128:q * 128 + 128],
                            src_tm[:, cc * 128:cc * 128 + 128], seyeb[:]))

                @blk.vector
                def _(e, b=b, hi=hi):
                    e.wait_ge(te.sem, te.n)
                    ve.inc(e.tensor_copy(dstap_fn(b, hi),
                                         tbs[b % 2][:, 0:hi * 128]))

        def fm4(buf):
            return buf[:].rearrange("p (cc j f) -> p cc j f", j=4, f=128)

        def fm512(buf):
            return buf[:].rearrange("p (cc r) -> p cc r", r=512)

        with ExitStack() as pctx:
            pE = pctx.enter_context
            fbuf = [pE(nc.sbuf_tensor(f"fbuf{i}", [128, 2048], F32))
                    for i in range(2)]
            tbuf = [pE(nc.sbuf_tensor(f"tbuf{i}", [128, 2048], BF16))
                    for i in range(2)]
            scols = pE(nc.sbuf_tensor("scols", [128, 64], F32))
            sbcw = pE(nc.sbuf_tensor("sbcw", [128, 1], F32))
            st1p = pE(nc.sbuf_tensor("st1p", [128, 2048], F32))
            rndp = pE(nc.sbuf_tensor("rndp", [128, 2048], F32))

            @blk.sync
            def _(e):
                for dst, src in [(sonescf, onescf), (sonesrf, onesrf),
                                 (sninvr, ninvr)]:
                    ld.inc(e.dma_start(out=dst[:], in_=src), 16)
            ld_c0 = ld.n

            @blk.vector
            def _(e):
                ve.inc(e.memset(swrow[:], 0.0))

            WLIST = [(fwq, twq, 16, 2048, 0), (fwk, twk, 16, 2048, 1),
                     (fwv, twv, 4, 8192, 2), (fwo, two, 4, 8192, 3),
                     (fwg, twg, 13, 8192, 4), (fwu, twu, 13, 8192, 5),
                     (fwd, twd, 52, 2048, 6)]
            for (fsrc, tdst, ns, wid, widx) in WLIST:
                panels = [(s, c0) for s in range(ns)
                          for c0 in range(0, wid, 2048)]
                np_ = len(panels)
                ld0, ve0 = ld.n, ve.n

                @blk.sync
                def _(e, fsrc=fsrc, panels=panels, ve0=ve0):
                    for i, (s, c0) in enumerate(panels):
                        e.wait_ge(ve.sem, ve0 + (i - 1) if i >= 2 else ve0)
                        ld.inc(e.dma_start(out=fbuf[i % 2][:],
                                           in_=fsrc[s][:, c0:c0 + 2048]), 16)

                @blk.vector
                def _(e, ld0=ld0, np_=np_):
                    e.wait_ge(ld.sem, ld_c0)
                    for i in range(np_):
                        e.wait_ge(ld.sem, ld0 + 16 * (i + 1))
                        ve.inc(e.reduce_sum(scols[:, i:i + 1], fbuf[i % 2][:],
                                            AX.X, apply_absolute_value=True))
                    e.wait_ge(ve.sem, ve.n)
                    ve.inc(e.reduce_sum(tt8[:, 0:1], scols[:, 0:np_], AX.X))

                @blk.tensor
                def _(e, widx=widx):
                    e.wait_ge(ve.sem, ve.n)
                    te.inc(e.matmul(psE[0:1, widx:widx + 1], tt8[:, 0:1],
                                    sonescf[:], start=True, stop=True))

                @blk.scalar
                def _(e, widx=widx):
                    e.wait_ge(te.sem, te.n)
                    se.inc(e.copy(swrow[0:1, widx:widx + 1],
                                  psE[0:1, widx:widx + 1]))

                @blk.vector
                def _(e, widx=widx):
                    e.wait_ge(se.sem, se.n)
                    ve.inc(e.tensor_tensor(tt8[0:1, 1:2],
                                           swrow[0:1, widx:widx + 1],
                                           sninvr[0:1, widx:widx + 1],
                                           ALU.mult))
                    e.wait_ge(ve.sem, ve.n)
                    ve.inc(e.tensor_scalar(tt8[0:1, 1:2], tt8[0:1, 1:2], 1e-5,
                                           None, ALU.max))

                @blk.scalar
                def _(e):
                    e.wait_ge(ve.sem, ve.n)
                    e.activation(tt8[0:1, 1:2], tt8[0:1, 1:2], ACT.Ln)
                    se.inc(e.activation(tt8[0:1, 1:2], tt8[0:1, 1:2], ACT.Exp,
                                        scale=-1.0))

                @blk.tensor
                def _(e):
                    e.wait_ge(se.sem, se.n)
                    te.inc(e.matmul(psD[:, 0:1], sonesrf[:], tt8[0:1, 1:2],
                                    start=True, stop=True))

                @blk.vector
                def _(e):
                    e.wait_ge(te.sem, te.n)
                    ve.inc(e.tensor_copy(sbcw[:], psD[:, 0:1]))

                ld1, ve1, gd1 = ld.n, ve.n, gd.n

                @blk.sync
                def _(e, fsrc=fsrc, panels=panels, ve1=ve1):
                    for i, (s, c0) in enumerate(panels):
                        e.wait_ge(ve.sem, ve1 + (i - 1) if i >= 2 else ve1)
                        ld.inc(e.dma_start(out=fbuf[i % 2][:],
                                           in_=fsrc[s][:, c0:c0 + 2048]), 16)

                @blk.vector
                def _(e, panels=panels, ld1=ld1, gd1=gd1):
                    for i in range(len(panels)):
                        e.wait_ge(ld.sem, ld1 + 16 * (i + 1))
                        if i >= 2:
                            e.wait_ge(gd.sem, gd1 + 16 * (i - 1))
                        e.tensor_scalar(st1p[:], fbuf[i % 2][:], sbcw[:, 0:1],
                                        CLIP, ALU.mult, ALU.min)
                        e.tensor_scalar(rndp[:].bitcast(I32), st1p[:], -CLIP,
                                        None, ALU.max)
                        ve.inc(e.tensor_copy(tbuf[i % 2][:],
                                             rndp[:].bitcast(I32)))

                @blk.gpsimd
                def _(e, tdst=tdst, panels=panels, ve1=ve1):
                    for i, (s, c0) in enumerate(panels):
                        e.wait_ge(ve.sem, ve1 + i + 1)
                        gd.inc(e.dma_start(out=tdst[s][:, c0:c0 + 2048],
                                           in_=tbuf[i % 2][:]), 16)

            barrier()

        @blk.vector
        def _(e):
            e.memset(sconst[:, 0:1], 1e-5)
            e.memset(sconst[:, 1:2], 1e-6)
            e.memset(sconst[:, 2:3], 1e-30)
            e.memset(sconst[:, 3:4], 0.0)
            e.memset(norm32[:], 0.0)
            ve.inc(e.memset(alj[:], 0.0))

        @blk.sync
        def _(e):
            for dst, src in [(seyeb, eyeb), (seyef, eyef), (sehs, ehs8)]:
                ld.inc(e.dma_start(out=dst[:], in_=src), 16)

        @blk.vector
        def _(e):
            ve.inc(e.tensor_tensor(srow8[:], swrow[:], sninvr[:],
                                   ALU.mult))
            e.wait_ge(ve.sem, ve.n)
            ve.inc(e.tensor_scalar(srow8[:], srow8[:], 1e-5, None, ALU.max))
            e.wait_ge(ve.sem, ve.n)
            ve.inc(e.tensor_scalar(srow8[:], srow8[:], 1.0 / 127.0, None,
                                   ALU.mult))

        @blk.tensor
        def _(e):
            e.wait_ge(ve.sem, ve.n)
            te.inc(e.matmul(psF[:, 0:8], sonesrf[:], srow8[:],
                            start=True, stop=True))

        @blk.vector
        def _(e):
            e.wait_ge(te.sem, te.n)
            ve.inc(e.tensor_copy(cwb[:], psF[:, 0:8]))
            e.wait_ge(ve.sem, ve.n)
            ve.inc(e.tensor_scalar(cwb[:, 0:1], cwb[:, 0:1], RSQDK, None,
                                   ALU.mult))

        barrier()

        with ExitStack() as actx:
            aE = actx.enter_context
            gb = {"o": gbo}
            for p in "qkv":
                gb[p] = aE(nc.sbuf_tensor("gb" + p, [128, D], BF16))
            xbuf = [aE(nc.sbuf_tensor(f"xbuf{i}", [128, D], F32)) for i in range(2)]
            xqfm = {p: aE(nc.sbuf_tensor("xqfm" + p, [128, NCD * 512], BF16))
                    for p in "qkv"}
            wsl = [aE(nc.sbuf_tensor(f"wsl{i}", [128, NCD * 128], BF16))
                   for i in range(2)]
            vsl = [aE(nc.sbuf_tensor(f"vsl{i}", [128, NCD * 512], BF16))
                   for i in range(2)]
            qob = [aE(nc.sbuf_tensor(f"qob{i}", [128, 512], BF16)) for i in range(2)]
            vob = [aE(nc.sbuf_tensor(f"vob{i}", [128, 512], BF16)) for i in range(2)]
            abc = aE(nc.sbuf_tensor("abc", [128, 512], F32))
            arow = aE(nc.sbuf_tensor("arow", [1, 512], F32))
            tb0 = aE(nc.psum_tensor("tbA0", [128, 1024], BF16))
            tb1 = aE(nc.psum_tensor("tbA1", [128, 1024], BF16))
            tbs = [tb0, tb1]

            ld_g0 = ld.n
            ve_g0 = ve.n

            @blk.sync
            def _(e):
                for i, p in enumerate("qkvo"):
                    e.wait_ge(ve.sem, ve_g0 + i)
                    ld.inc(e.dma_start(
                        out=xbuf[0][:],
                        in_=g_in[p][None, :].partition_broadcast(128)), 16)

            @blk.vector
            def _(e):
                for i, p in enumerate("qkvo"):
                    e.wait_ge(ld.sem, ld_g0 + (i + 1) * 16)
                    ve.inc(e.tensor_copy(gb[p][:], xbuf[0][:]))

            barrier()

            for g in range(NGA):
                own = g < NGO
                projs = "qkv" if own else "kv"
                for j in range(4):
                    tt = g * 4 + j

                    @blk.sync
                    def _(e, j=j, tt=tt):
                        e.wait_ge(se.sem, se.n)
                        ld.inc(e.dma_start(out=xbuf[j % 2][:],
                                           in_=x_in[tt * 128:(tt + 1) * 128, :]),
                               16)

                    @blk.vector
                    def _(e):
                        e.wait_ge(ld.sem, ld.n)
                        ve.inc(e.engine_nop())

                    emit_ln(xbuf[j % 2])
                    for p in projs:
                        pi = "qkv".index(p)
                        emit_quant(xn, D, gb[p], True, "qkvo".index(p),
                                   pi * 4 + j)
                        emit_transposes(
                            tbs, xqtm, NCD,
                            lambda b, hi, p=p, j=j: fm4(xqfm[p])
                                [:, b * 8:b * 8 + hi, j, :])

                for p in ("qk" if own else "k"):
                    pi = "qkv".index(p)

                    @blk.tensor
                    def _(e, pi=pi):
                        e.wait_ge(ve.sem, ve.n)
                        for j in range(4):
                            te.inc(e.transpose(psF[0:1, j * 128:j * 128 + 128],
                                               alj[:, pi * 4 + j:pi * 4 + j + 1],
                                               seyef[:]))

                    @blk.vector
                    def _(e):
                        e.wait_ge(te.sem, te.n)
                        ve.inc(e.tensor_copy(arow[:], psF[0:1, 0:512]))

                    @blk.tensor
                    def _(e):
                        e.wait_ge(ve.sem, ve.n)
                        te.inc(e.matmul(psF[:, 0:512], sonesrf[:], arow[:],
                                        start=True, stop=True))

                    @blk.vector
                    def _(e):
                        e.wait_ge(te.sem, te.n)
                        ve.inc(e.tensor_copy(abc[:], psF[:, 0:512]))

                    tw = twq if p == "q" else twk
                    ld0, te0, ve0, gd0 = ld.n, te.n, ve.n, gd.n
                    ps = [psA, psB]

                    @blk.sync
                    def _(e, tw=tw, ld0=ld0, te0=te0):
                        for s in range(16):
                            e.wait_ge(te.sem, te0 + (s - 1) if s >= 2 else te0)
                            ld.inc(e.dma_start(out=wsl[s % 2][:], in_=tw[s]), 16)

                    @blk.tensor
                    def _(e, p=p, ld0=ld0, ve0=ve0):
                        for s in range(16):
                            e.wait_ge(ld.sem, ld0 + 16 * (s + 1))
                            e.wait_ge(ve.sem, ve0 + (s - 1) if s >= 2 else ve0)
                            for cc in range(NCD):
                                mm = e.matmul(
                                    ps[s % 2][:, 0:512],
                                    wsl[s % 2][:, cc * 128:cc * 128 + 128],
                                    fm512(xqfm[p])[:, cc, :],
                                    start=(cc == 0), stop=(cc == NCD - 1))
                            te.inc(mm)

                    @blk.vector
                    def _(e, te0=te0, gd0=gd0):
                        for s in range(16):
                            e.wait_ge(te.sem, te0 + s + 1)
                            e.wait_ge(gd.sem, gd0 + 16 * (s - 1) if s >= 2 else gd0)
                            ve.inc(e.tensor_tensor(qob[s % 2][:],
                                                   ps[s % 2][:, 0:512],
                                                   abc[:], ALU.mult))

                    @blk.gpsimd
                    def _(e, p=p, g=g, ve0=ve0):
                        for s in range(16):
                            e.wait_ge(ve.sem, ve0 + s + 1)
                            if p == "q":
                                dst = qscr[g, s]
                            else:
                                dst = kscr[s][:, g * 512:(g + 1) * 512]
                            gd.inc(e.dma_start(out=dst, in_=qob[s % 2][:]), 16)

                ld0, te0, se0, gd0 = ld.n, te.n, se.n, gd.n
                psv = [psC, psD]

                @blk.sync
                def _(e, ld0=ld0, te0=te0):
                    for ob in range(4):
                        if ob >= 2:
                            e.wait_ge(te.sem, te0 + (ob - 1) * 4)
                        ld.inc(e.dma_start(out=vsl[ob % 2][:], in_=twv[ob]), 16)

                @blk.tensor
                def _(e, ld0=ld0, se0=se0):
                    for ob in range(4):
                        e.wait_ge(ld.sem, ld0 + 16 * (ob + 1))
                        for j in range(4):
                            m = ob * 4 + j
                            if m >= 2:
                                e.wait_ge(se.sem, se0 + (m - 1))
                            for cc in range(NCD):
                                mm = e.matmul(
                                    psv[m % 2][:, 0:512],
                                    fm4(xqfm["v"])[:, cc, j, :],
                                    fm512(vsl[ob % 2])[:, cc, :],
                                    start=(cc == 0), stop=(cc == NCD - 1))
                            te.inc(mm)

                @blk.scalar
                def _(e, te0=te0, gd0=gd0):
                    for ob in range(4):
                        for j in range(4):
                            m = ob * 4 + j
                            e.wait_ge(te.sem, te0 + m + 1)
                            if m >= 2:
                                e.wait_ge(gd.sem, gd0 + 64 * (m - 1))
                            se.inc(e.activation(vob[m % 2][:],
                                                psv[m % 2][:, 0:512], ACT.Copy,
                                                scale=alj[:, 8 + j:8 + j + 1]))

                @blk.gpsimd
                def _(e, g=g, se0=se0):
                    for ob in range(4):
                        for j in range(4):
                            m = ob * 4 + j
                            tt = g * 4 + j
                            e.wait_ge(se.sem, se0 + m + 1)
                            for q in range(4):
                                gd.inc(e.dma_start(
                                    out=vscr[4 * ob + q][:, tt, :],
                                    in_=vob[m % 2][:, q * 128:q * 128 + 128]), 16)

                barrier()

        barrier()

        with ExitStack() as bctx:
            bE = bctx.enter_context
            kb0 = bE(nc.sbuf_tensor("kb0", [128, TB], BF16))
            k2 = bE(nc.sbuf_tensor("k2", [128, TB], BF16))
            for h in range(H):
                @blk.sync
                def _(e, h=h):
                    e.wait_ge(ve.sem, ve.n)
                    ld.inc(e.dma_start(out=kb0[:], in_=kscr[h]), 16)

                @blk.vector
                def _(e):
                    e.wait_ge(ld.sem, ld.n)
                    e.wait_ge(te.sem, te.n)
                    ve.inc(e.tensor_tensor(k2[:], kb0[:], kb0[:], ALU.mult))

                @blk.tensor
                def _(e):
                    e.wait_ge(ve.sem, ve.n)
                    for jj in range(8):
                        mm = e.matmul(psA[:, 0:512],
                                      sehs[:, jj * 128:jj * 128 + 128],
                                      k2[:, jj * 512:jj * 512 + 512],
                                      start=(jj == 0), stop=(jj == 7))
                    te.inc(mm)

                @blk.vector
                def _(e, h=h):
                    e.wait_ge(te.sem, te.n)
                    ve.inc(e.reduce_max(norm32[0:8, 0:1], psA[0:8, 0:512],
                                        AX.X))
                    e.wait_ge(ve.sem, ve.n)
                    ve.inc(e.transpose(normT[:], norm32[:]))
                    e.wait_ge(ve.sem, ve.n)
                    ve.inc(e.reduce_max(km2row[0:1, h:h + 1], normT[0:1, 0:8],
                                        AX.X))

        barrier()

        for g in range(NGO):
            with ExitStack() as gctx:
                gE = gctx.enter_context
                xqpfm = gE(nc.sbuf_tensor(f"xqpfm{g}", [128, NCD * 512], BF16))
                sC = ExitStack()
                v1 = [sC.enter_context(nc.sbuf_tensor(f"v1_{g}_{j}", [128, D], F32))
                      for j in range(4)]
                ao = [sC.enter_context(nc.sbuf_tensor(f"ao{g}{j}", [128, D], F32))
                      for j in range(4)]
                sB = ExitStack()
                attnout = sB.enter_context(
                    nc.sbuf_tensor(f"attnout{g}", [128, H * 512], BF16))
                negbc = sB.enter_context(nc.sbuf_tensor(f"negbc{g}", [128, 16], F32))
                negrow = sB.enter_context(nc.sbuf_tensor(f"negrow{g}", [1, 16], F32))
                ssbA = sB.enter_context(nc.sbuf_tensor(f"ssbA{g}", [8, 512], F32))
                ssbB = sB.enter_context(nc.sbuf_tensor(f"ssbB{g}", [8, 512], F32))
                rec = sB.enter_context(nc.sbuf_tensor(f"rec{g}", [128, 64], F32))
                sA = ExitStack()
                qgrp = sA.enter_context(
                    nc.sbuf_tensor(f"qgrp{g}", [128, H * 512], BF16))
                kb = [sA.enter_context(nc.sbuf_tensor(f"kb{g}{i}", [128, TB], BF16))
                      for i in range(2)]
                vb = [sA.enter_context(nc.sbuf_tensor(f"vb{g}{i}", [128, TB], BF16))
                      for i in range(2)]
                pbuf = sA.enter_context(
                    nc.sbuf_tensor(f"pbuf{g}", [128, 1024], BF16))

                @blk.sync
                def _(e, g=g):
                    e.wait_ge(gd.sem, gd.n)
                    for j in range(4):
                        tt = g * 4 + j
                        ld.inc(e.dma_start(
                            out=v1[j][:], in_=x_in[tt * 128:(tt + 1) * 128, :]), 16)
                    ld.inc(e.dma_start(
                        out=qgrp[:].rearrange("p (h f) -> p h f", f=512),
                        in_=qscr[g].rearrange("h p f -> p h f")), 16)

                @blk.vector
                def _(e):
                    e.wait_ge(ld.sem, ld.n)
                    ve.inc(e.tensor_tensor(attnout[:], qgrp[:], qgrp[:],
                                           ALU.mult))

                @blk.tensor
                def _(e):
                    e.wait_ge(ve.sem, ve.n)
                    for h in range(H):
                        bank = psA if h < 8 else psB
                        mm = e.matmul(bank[:, 0:512],
                                      sehs[:, (h % 8) * 128:(h % 8) * 128 + 128],
                                      attnout[:, h * 512:h * 512 + 512],
                                      start=(h % 8 == 0), stop=(h % 8 == 7))
                        if h % 8 == 7:
                            te.inc(mm)

                @blk.vector
                def _(e):
                    e.wait_ge(te.sem, te.n)
                    ve.inc(e.reduce_max(norm32[0:8, 0:1], psA[0:8, 0:512],
                                        AX.X))
                    e.wait_ge(ve.sem, ve.n)
                    ve.inc(e.transpose(normT[:], norm32[:]))
                    e.wait_ge(ve.sem, ve.n)
                    e.tensor_copy(qrow[0:1, 0:8], normT[0:1, 0:8])
                    ve.inc(e.reduce_max(norm32[0:8, 0:1], psB[0:8, 0:512],
                                        AX.X))
                    e.wait_ge(ve.sem, ve.n)
                    ve.inc(e.transpose(normT[:], norm32[:]))
                    e.wait_ge(ve.sem, ve.n)
                    ve.inc(e.tensor_copy(qrow[0:1, 8:16], normT[0:1, 0:8]))
                    e.wait_ge(ve.sem, ve.n)
                    ve.inc(e.tensor_tensor(negrow[:], qrow[:],
                                           km2row[:], ALU.mult))

                @blk.scalar
                def _(e):
                    e.wait_ge(ve.sem, ve.n)
                    e.activation(negrow[:], negrow[:], ACT.Ln,
                                 bias=sconst[0:1, 2:3])
                    se.inc(e.activation(negrow[:], negrow[:], ACT.Exp, scale=0.5))

                @blk.vector
                def _(e):
                    e.wait_ge(se.sem, se.n)
                    ve.inc(e.tensor_scalar(negrow[:], negrow[:], -1.001, None,
                                           ALU.mult))

                @blk.tensor
                def _(e):
                    e.wait_ge(ve.sem, ve.n)
                    te.inc(e.matmul(psF[:, 0:16], sonesrf[:], negrow[:],
                                    start=True, stop=True))

                @blk.vector
                def _(e):
                    e.wait_ge(te.sem, te.n)
                    ve.inc(e.tensor_copy(negbc[:], psF[:, 0:16]))

                barrier()

                ld0, te0, se0, ve0 = ld.n, te.n, se.n, ve.n
                teh = lambda h: te0 + 64 * h

                @blk.sync
                def _(e, ld0=ld0, te0=te0):
                    for h in range(H):
                        if h >= 2:
                            e.wait_ge(te.sem, teh(h - 1))
                        ld.inc(e.dma_start(out=kb[h % 2][:], in_=kscr[h]), 16)
                        ld.inc(e.dma_start(
                            out=vb[h % 2][:],
                            in_=vscr[h].rearrange("p t f -> p (t f)")), 16)

                @blk.tensor
                def _(e, ld0=ld0, se0=se0, ve0=ve0):
                    for h in range(H):
                        e.wait_ge(ld.sem, ld0 + 32 * (h + 1))
                        e.wait_ge(se.sem, se0 + 32 * h)
                        e.wait_ge(ve.sem, ve0 + h)
                        for kc in range(2):
                            te.inc(e.matmul(
                                [psA, psB][kc][:, 0:512],
                                kb[h % 2][:, kc * 128:kc * 128 + 128],
                                qgrp[:, h * 512:h * 512 + 512],
                                start=True, stop=True))
                        for kc in range(32):
                            e.wait_ge(se.sem, se0 + 32 * h + kc + 1)
                            e.matmul(psC[:, 0:512],
                                     vb[h % 2][:, kc * 128:kc * 128 + 128],
                                     pbuf[:, (kc % 2) * 512:(kc % 2) * 512 + 512],
                                     start=(kc == 0), stop=(kc == 31))
                            bank = psD if h < 8 else psE
                            te.inc(e.matmul(
                                bank[:, 0:512],
                                sehs[:, (h % 8) * 128:(h % 8) * 128 + 128],
                                pbuf[:, (kc % 2) * 512:(kc % 2) * 512 + 512],
                                start=(h % 8 == 0 and kc == 0),
                                stop=(h % 8 == 7 and kc == 31)))
                            if kc + 2 < 32:
                                te.inc(e.matmul(
                                    [psA, psB][kc % 2][:, 0:512],
                                    kb[h % 2][:, (kc + 2) * 128:(kc + 2) * 128 + 128],
                                    qgrp[:, h * 512:h * 512 + 512],
                                    start=True, stop=True))

                @blk.scalar
                def _(e, te0=te0):
                    for h in range(H):
                        for kc in range(32):
                            if kc < 2:
                                tev = teh(h) + kc + 1
                            else:
                                tev = teh(h) + 2 * kc
                            e.wait_ge(te.sem, tev)
                            se.inc(e.activation(
                                pbuf[:, (kc % 2) * 512:(kc % 2) * 512 + 512],
                                [psA, psB][kc % 2][:, 0:512], ACT.Exp,
                                bias=negbc[:, h:h + 1]))

                @blk.vector
                def _(e, te0=te0):
                    for h in range(H):
                        e.wait_ge(te.sem, teh(h + 1))
                        ve.inc(e.tensor_copy(attnout[:, h * 512:h * 512 + 512],
                                             psC[:, 0:512]))

                barrier()
                sA.close()

                @blk.vector
                def _(e):
                    e.tensor_copy(ssbA[:], psD[0:8, 0:512])
                    ve.inc(e.tensor_copy(ssbB[:], psE[0:8, 0:512]))

                @blk.tensor
                def _(e):
                    e.wait_ge(ve.sem, ve.n)
                    for j in range(4):
                        e.transpose(psF[:, j * 16:j * 16 + 8],
                                    ssbA[0:8, j * 128:j * 128 + 128],
                                    seyef[0:8, 0:8])
                        te.inc(e.transpose(psF[:, j * 16 + 8:j * 16 + 16],
                                           ssbB[0:8, j * 128:j * 128 + 128],
                                           seyef[0:8, 0:8]))

                @blk.scalar
                def _(e):
                    e.wait_ge(te.sem, te.n)
                    e.activation(rec[:], psF[:, 0:64], ACT.Ln)
                    se.inc(e.activation(rec[:], rec[:], ACT.Exp, scale=-1.0))

                with nc.psum_tensor(f"tbB0{g}", [128, 1024], BF16) as tg0, \
                     nc.psum_tensor(f"tbB1{g}", [128, 1024], BF16) as tg1:
                    tgs = [tg0, tg1]
                    for j in range(4):
                        for hb in range(2):
                            @blk.tensor
                            def _(e, j=j, hb=hb):
                                e.wait_ge(ve.sem, ve.n)
                                for q in range(8):
                                    h = hb * 8 + q
                                    te.inc(e.transpose(
                                        tgs[hb][:, q * 128:q * 128 + 128],
                                        attnout[:, h * 512 + j * 128:
                                                h * 512 + j * 128 + 128],
                                        seyeb[:]))

                            @blk.vector
                            def _(e, j=j, hb=hb):
                                e.wait_ge(te.sem, te.n)
                                e.wait_ge(se.sem, se.n)
                                ins = None
                                for q in range(8):
                                    h = hb * 8 + q
                                    ins = e.tensor_scalar(
                                        ao[j][:, h * 128:h * 128 + 128],
                                        tgs[hb][:, q * 128:q * 128 + 128],
                                        rec[:, j * 16 + h:j * 16 + h + 1],
                                        None, ALU.mult)
                                ve.inc(ins)

                    sB.close()
                    for j in range(4):
                        emit_rms(ao[j])
                        emit_quant(ao[j], D, gbo, True, 3, j)
                        emit_transposes(
                            tgs, xqtm, NCD,
                            lambda b, hi, j=j: fm4(xqpfm)
                                [:, b * 8:b * 8 + hi, j, :])

                barrier()

                sD = ExitStack()
                osl = [sD.enter_context(
                    nc.sbuf_tensor(f"osl{g}{i}", [128, NCD * 512], BF16))
                    for i in range(2)]
                ld0, te0, ve0 = ld.n, te.n, ve.n
                pso = [psA, psB]

                @blk.sync
                def _(e, ld0=ld0, te0=te0):
                    for ob in range(4):
                        if ob >= 2:
                            e.wait_ge(te.sem, te0 + (ob - 1) * 4)
                        ld.inc(e.dma_start(out=osl[ob % 2][:], in_=two[ob]), 16)

                @blk.tensor
                def _(e, ld0=ld0, ve0=ve0):
                    for ob in range(4):
                        e.wait_ge(ld.sem, ld0 + 16 * (ob + 1))
                        for j in range(4):
                            m = ob * 4 + j
                            if m >= 2:
                                e.wait_ge(ve.sem, ve0 + (m - 1))
                            for cc in range(NCD):
                                mm = e.matmul(
                                    pso[m % 2][:, 0:512],
                                    fm4(xqpfm)[:, cc, j, :],
                                    fm512(osl[ob % 2])[:, cc, :],
                                    start=(cc == 0), stop=(cc == NCD - 1))
                            te.inc(mm)

                @blk.vector
                def _(e, te0=te0):
                    for ob in range(4):
                        for j in range(4):
                            m = ob * 4 + j
                            e.wait_ge(te.sem, te0 + m + 1)
                            ve.inc(e.scalar_tensor_tensor(
                                v1[j][:, ob * 512:ob * 512 + 512],
                                pso[m % 2][:, 0:512], alj[:, j:j + 1],
                                v1[j][:, ob * 512:ob * 512 + 512],
                                ALU.mult, ALU.add))

                @blk.gpsimd
                def _(e, g=g, ve0=ve0):
                    for j in range(4):
                        e.wait_ge(ve.sem, ve0 + 13 + j)
                        row = (g * 4 + j) * 128
                        gd.inc(e.dma_start(out=x1scr[row:row + 128, :],
                                           in_=v1[j][:]), 16)

                barrier()
                sD.close()
                sC.close()

                sE = ExitStack()
                gsl = [sE.enter_context(
                    nc.sbuf_tensor(f"gsl{g}{i}", [128, NCD * 512], BF16))
                    for i in range(2)]
                usl = [sE.enter_context(
                    nc.sbuf_tensor(f"usl{g}{i}", [128, NCD * 512], BF16))
                    for i in range(2)]
                gu = [sE.enter_context(
                    nc.sbuf_tensor(f"gu{g}{j}", [128, IP], BF16))
                    for j in range(2)]
                gtmp = sE.enter_context(nc.sbuf_tensor(f"gtmp{g}", [128, 512], BF16))
                utmp = sE.enter_context(nc.sbuf_tensor(f"utmp{g}", [128, 512], BF16))
                guqfm = sE.enter_context(
                    nc.sbuf_tensor(f"guqfm{g}", [128, NCI * 256], BF16))
                dch = [sE.enter_context(
                    nc.sbuf_tensor(f"dch{g}{i}", [128, 2048], BF16))
                    for i in range(2)]
                otmp = [sE.enter_context(
                    nc.sbuf_tensor(f"otmp{g}{i}", [128, 512], F32))
                    for i in range(2)]
                x1b = [sE.enter_context(
                    nc.sbuf_tensor(f"x1b{g}{i}", [128, D], F32))
                    for i in range(2)]

                for pr in range(2):
                    with nc.psum_tensor(f"tbC0{g}{pr}", [128, 1024], BF16) as tc0, \
                         nc.psum_tensor(f"tbC1{g}{pr}", [128, 1024], BF16) as tc1:
                        tcs = [tc0, tc1]
                        for j2 in range(2):
                            t = pr * 2 + j2

                            @blk.sync
                            def _(e, g=g, t=t, j2=j2):
                                e.wait_ge(ve.sem, ve.n)
                                row = (g * 4 + t) * 128
                                ld.inc(e.dma_start(out=x1b[j2][:],
                                                   in_=x1scr[row:row + 128, :]),
                                       16)

                            @blk.vector
                            def _(e):
                                e.wait_ge(ld.sem, ld.n)
                                ve.inc(e.engine_nop())

                            emit_ln(x1b[j2])
                            emit_quant(xn, D, None, False, 4, j2,
                                       extra_acols=[(5, 4 + j2)])
                            emit_transposes(
                                tcs, xqtm, NCD,
                                lambda b, hi, j2=j2: fm4(xqpfm)
                                    [:, b * 8:b * 8 + hi, j2, :])

                        barrier()

                    ld0, te0, se0, ve0 = ld.n, te.n, se.n, ve.n
                    pg = [psA, psB]
                    pu = [psC, psD]

                    @blk.sync
                    def _(e, ld0=ld0, te0=te0):
                        for hb in range(13):
                            e.wait_ge(te.sem,
                                      te0 + (hb - 1) * 2 if hb >= 2 else te0)
                            ld.inc(e.dma_start(out=gsl[hb % 2][:], in_=twg[hb]),
                                   16)
                            ld.inc(e.dma_start(out=usl[hb % 2][:], in_=twu[hb]),
                                   16)

                    @blk.tensor
                    def _(e, ld0=ld0, se0=se0):
                        for hb in range(13):
                            e.wait_ge(ld.sem, ld0 + 32 * (hb + 1))
                            for j2 in range(2):
                                m = hb * 2 + j2
                                e.wait_ge(se.sem,
                                          se0 + (m - 1) if m >= 2 else se0)
                                for cc in range(NCD):
                                    xq = fm4(xqpfm)[:, cc, j2, :]
                                    e.matmul(
                                        pg[m % 2][:, 0:512], xq,
                                        fm512(gsl[hb % 2])[:, cc, :],
                                        start=(cc == 0), stop=(cc == NCD - 1))
                                    mm = e.matmul(
                                        pu[m % 2][:, 0:512], xq,
                                        fm512(usl[hb % 2])[:, cc, :],
                                        start=(cc == 0), stop=(cc == NCD - 1))
                                te.inc(mm)

                    @blk.scalar
                    def _(e, te0=te0, ve0=ve0):
                        for hb in range(13):
                            for j2 in range(2):
                                m = hb * 2 + j2
                                e.wait_ge(te.sem, te0 + m + 1)
                                if m >= 1:
                                    e.wait_ge(ve.sem, ve0 + m)
                                e.activation(gtmp[:], pg[m % 2][:, 0:512],
                                             ACT.Silu, scale=alj[:, j2:j2 + 1])
                                se.inc(e.activation(utmp[:], pu[m % 2][:, 0:512],
                                                    ACT.Copy,
                                                    scale=alj[:, 4 + j2:5 + j2]))

                    @blk.vector
                    def _(e, se0=se0):
                        for hb in range(13):
                            for j2 in range(2):
                                m = hb * 2 + j2
                                e.wait_ge(se.sem, se0 + m + 1)
                                ve.inc(e.tensor_tensor(
                                    gu[j2][:, hb * 512:hb * 512 + 512],
                                    gtmp[:], utmp[:], ALU.mult))

                    barrier()

                    with nc.psum_tensor(f"tbD0{g}{pr}", [128, 1024], BF16) as td0, \
                         nc.psum_tensor(f"tbD1{g}{pr}", [128, 1024], BF16) as td1:
                        tds = [td0, td1]
                        for j2 in range(2):
                            t = pr * 2 + j2

                            @blk.vector
                            def _(e, j2=j2):
                                e.wait_ge(se.sem, se.n)
                                e.wait_ge(te.sem, te.n)
                                ve.inc(e.reduce_max(
                                    tt8[:, 4:5], gu[j2][:], AX.X,
                                    apply_absolute_value=True))
                                e.wait_ge(ve.sem, ve.n)
                                ve.inc(e.tensor_scalar(tt8[:, 4:5],
                                                       tt8[:, 4:5], 1e-5,
                                                       None, ALU.max))

                            @blk.scalar
                            def _(e):
                                e.wait_ge(ve.sem, ve.n)
                                e.activation(tt8[:, 6:7], tt8[:, 4:5], ACT.Ln)
                                se.inc(e.activation(tt8[:, 6:7], tt8[:, 6:7],
                                                    ACT.Exp, scale=-1.0))

                            @blk.vector
                            def _(e, t=t, j2=j2):
                                e.wait_ge(se.sem, se.n)
                                for q in range(4):
                                    lo = q * 1664
                                    e.tensor_scalar(
                                        rnd32[:, 0:1664].bitcast(I32),
                                        gu[j2][:, lo:lo + 1664], tt8[:, 6:7],
                                        127.0, ALU.mult, ALU.mult)
                                    e.tensor_copy(gu[j2][:, lo:lo + 1664],
                                                  rnd32[:, 0:1664].bitcast(I32))
                                ve.inc(e.tensor_tensor(
                                    alj[:, 8 + t:8 + t + 1], tt8[:, 4:5],
                                    cwb[:, 6:7], ALU.mult))

                            emit_transposes(
                                tds, gu[j2], NCI,
                                lambda b, hi, j2=j2: guqfm[:].rearrange(
                                    "p (cc j f) -> p cc j f", j=2, f=128)
                                    [:, b * 8:b * 8 + hi, j2, :])

                        barrier()

                    with nc.psum_tensor(f"pdG{g}{pr}", [128, 512], F32) as psG, \
                         nc.psum_tensor(f"pdH{g}{pr}", [128, 512], F32) as psH:
                        pd = [psA, psB, psC, psD, psE, psF, psG, psH]
                        ld0, te0, ve0, gd0 = ld.n, te.n, ve.n, gd.n

                        @blk.sync
                        def _(e, ld0=ld0, te0=te0):
                            for ci in range(NCI):
                                if ci >= 2:
                                    e.wait_ge(te.sem, te0 + (ci - 1))
                                ld.inc(e.dma_start(out=dch[ci % 2][:],
                                                   in_=twd[ci]), 16)

                        @blk.tensor
                        def _(e, ld0=ld0):
                            for ci in range(NCI):
                                e.wait_ge(ld.sem, ld0 + 16 * (ci + 1))
                                for j2 in range(2):
                                    for ob in range(4):
                                        mm = e.matmul(
                                            pd[j2 * 4 + ob][:, 0:512],
                                            guqfm[:].rearrange(
                                                "p (cc j f) -> p cc j f",
                                                j=2, f=128)[:, ci, j2, :],
                                            dch[ci % 2][:, ob * 512:
                                                        ob * 512 + 512],
                                            start=(ci == 0),
                                            stop=(ci == NCI - 1))
                                te.inc(mm)

                        @blk.vector
                        def _(e, g=g, pr=pr, te0=te0, gd0=gd0):
                            e.wait_ge(te.sem, te0 + NCI)
                            for j2 in range(2):
                                t = pr * 2 + j2
                                for ob in range(4):
                                    m = j2 * 4 + ob
                                    if m >= 2:
                                        e.wait_ge(gd.sem, gd0 + 16 * (m - 1))
                                    ve.inc(e.scalar_tensor_tensor(
                                        otmp[m % 2][:],
                                        pd[m][:, 0:512],
                                        alj[:, 8 + t:8 + t + 1],
                                        x1b[j2][:, ob * 512:ob * 512 + 512],
                                        ALU.mult, ALU.add))

                        @blk.gpsimd
                        def _(e, g=g, pr=pr, ve0=ve0):
                            for j2 in range(2):
                                t = pr * 2 + j2
                                row = (g * 4 + t) * 128
                                for ob in range(4):
                                    m = j2 * 4 + ob
                                    e.wait_ge(ve.sem, ve0 + m + 1)
                                    gd.inc(e.dma_start(
                                        out=out[row:row + 128,
                                                ob * 512:ob * 512 + 512],
                                        in_=otmp[m % 2][:]), 16)

                    barrier()
                sE.close()

        @blk.gpsimd
        def _(e):
            e.wait_ge(gd.sem, gd.n)

    return nc



def kernel(L0_attn_x, L0_attn_wq, L0_attn_wk, L0_attn_wv, L0_attn_wo,
           L0_attn_gq, L0_attn_gk, L0_attn_gv, L0_attn_go,
           L0_ffn_w_gate, L0_ffn_w_up, L0_ffn_w_down):
    f32 = np.float32
    x = np.ascontiguousarray(np.asarray(L0_attn_x, f32))

    def lay_dd(w):
        a = np.asarray(w, f32).reshape(16, 128, 16, 128)
        return np.ascontiguousarray(a.transpose(0, 3, 2, 1)).reshape(
            16, 128, 2048)

    def lay_512(w):
        a = np.asarray(w, f32).reshape(4, 4, 128, 16, 128)
        return np.ascontiguousarray(a.transpose(0, 4, 3, 1, 2)).reshape(
            4, 128, NCD * 512)

    def lay_gu(w):
        wp = np.zeros((IP, D), f32)
        wp[0:IREAL] = np.asarray(w, f32)
        a = wp.reshape(13, 4, 128, 16, 128)
        return np.ascontiguousarray(a.transpose(0, 4, 3, 1, 2)).reshape(
            13, 128, NCD * 512)

    def lay_dn(w):
        wp = np.zeros((D, IP), f32)
        wp[:, 0:IREAL] = np.asarray(w, f32)
        a = wp.reshape(16, 128, 52, 128)
        return np.ascontiguousarray(a.transpose(2, 3, 0, 1)).reshape(
            52, 128, 2048)

    fwq_h = lay_dd(L0_attn_wq)
    fwk_h = lay_dd(L0_attn_wk)
    fwv_h = lay_512(L0_attn_wv)
    fwo_h = lay_512(L0_attn_wo)
    fwg_h = lay_gu(L0_ffn_w_gate)
    fwu_h = lay_gu(L0_ffn_w_up)
    fwd_h = lay_dn(L0_ffn_w_down)

    eyefn = np.eye(128, dtype=f32)
    eyebn = np.eye(128, dtype=f32).astype(BF16NP)
    onescf = np.ones((128, 1), f32)
    onesrf = np.ones((1, 128), f32)
    ehs = np.zeros((128, 8 * 128), f32)
    for jj in range(8):
        ehs[:, jj * 128 + jj] = 1.0
    ehs = ehs.astype(BF16NP)
    ninvr = np.array([1.0 / n for n in NUMEL] + [1.0], f32)[0:8].reshape(1, 8)

    nc = build_main()
    in_maps = []
    for r in range(8):
        b = r // 2
        xb = x[b]
        if r % 2 == 1:
            xb = np.concatenate([xb[2048:], xb[:2048]], axis=0)
        in_maps.append(dict(
            x=np.ascontiguousarray(xb), fwq=fwq_h, fwk=fwk_h, fwv=fwv_h,
            fwo=fwo_h, fwg=fwg_h, fwu=fwu_h, fwd=fwd_h,
            ninvr=ninvr,
            gq=np.asarray(L0_attn_gq, f32), gk=np.asarray(L0_attn_gk, f32),
            gv=np.asarray(L0_attn_gv, f32), go=np.asarray(L0_attn_go, f32),
            eyeb=eyebn, eyef=eyefn, onescf=onescf, onesrf=onesrf, ehs8=ehs))
    res = run_bass_kernel_spmd(nc, in_maps, core_ids=list(range(8)))

    outp = np.empty((4, TB, D), np.float32)
    for b in range(4):
        outp[b, 0:2048] = np.asarray(res.results[2 * b]["out"], np.float32)
        outp[b, 2048:] = np.asarray(res.results[2 * b + 1]["out"], np.float32)
    return outp


# version 10
# speedup vs baseline: 1.0457x; 1.0457x over previous
import math
import numpy as np
import ml_dtypes
from contextlib import ExitStack

import concourse.bass as bass
import concourse.mybir as mybir
from concourse.bass_utils import run_bass_kernel_spmd

F32 = mybir.dt.float32
BF16 = mybir.dt.bfloat16
I32 = mybir.dt.int32
ALU = mybir.AluOpType
ACT = mybir.ActivationFunctionType
AX = mybir.AxisListType
BF16NP = ml_dtypes.bfloat16

D = 2048
H = 16
DK = 128
TB = 4096
TOWN = 2048
IREAL = 6464
IP = 6656
NCD = 16
NCI = 52
NGA = 8
NGO = 4
RSQDK = 1.0 / math.sqrt(DK)
CLIP = 1.4999
NUMEL = [D * D] * 4 + [IREAL * D] * 2 + [D * IREAL]


class Ctr:

    def __init__(self, sem):
        self.sem = sem
        self.n = 0

    def inc(self, inst, k=1):
        inst.then_inc(self.sem, k)
        self.n += k
        return self.n



def build_main():
    nc = bass.Bass("TRN2", target_bir_lowering=False)
    dram = lambda n, s, dt, k: nc.dram_tensor(n, s, dt, kind=k).ap()

    x_in = dram("x", [TB, D], F32, "ExternalInput")
    fwq = dram("fwq", [16, 128, NCD * 128], F32, "ExternalInput")
    fwk = dram("fwk", [16, 128, NCD * 128], F32, "ExternalInput")
    fwv = dram("fwv", [4, 128, NCD * 512], F32, "ExternalInput")
    fwo = dram("fwo", [4, 128, NCD * 512], F32, "ExternalInput")
    fwg = dram("fwg", [13, 128, NCD * 512], F32, "ExternalInput")
    fwu = dram("fwu", [13, 128, NCD * 512], F32, "ExternalInput")
    fwd = dram("fwd", [NCI, 128, 2048], F32, "ExternalInput")
    twq = dram("twq", [16, 128, NCD * 128], BF16, "Internal")
    twk = dram("twk", [16, 128, NCD * 128], BF16, "Internal")
    twv = dram("twv", [4, 128, NCD * 512], BF16, "Internal")
    two = dram("two", [4, 128, NCD * 512], BF16, "Internal")
    twg = dram("twg", [13, 128, NCD * 512], BF16, "Internal")
    twu = dram("twu", [13, 128, NCD * 512], BF16, "Internal")
    twd = dram("twd", [NCI, 128, 2048], BF16, "Internal")
    ninvr = dram("ninvr", [1, 8], F32, "ExternalInput")
    g_in = {p: dram("g" + p, [D], F32, "ExternalInput") for p in "qkvo"}
    eyeb = dram("eyeb", [128, 128], BF16, "ExternalInput")
    eyef = dram("eyef", [128, 128], F32, "ExternalInput")
    onescf = dram("onescf", [128, 1], F32, "ExternalInput")
    onesrf = dram("onesrf", [1, 128], F32, "ExternalInput")
    ehs8 = dram("ehs8", [128, 8 * 128], BF16, "ExternalInput")
    out = dram("out", [TOWN, D], F32, "ExternalOutput")

    x1scr = dram("x1scr", [TOWN, D], F32, "Internal")
    qscr = dram("qscr", [NGO, H, 128, 512], BF16, "Internal")
    kscr = dram("kscr", [H, 128, TB], BF16, "Internal")
    vscr = dram("vscr", [H, 128, 32, 128], BF16, "Internal")

    ctx = ExitStack()
    E = ctx.enter_context
    with ctx:
        gbo = E(nc.sbuf_tensor("gbo", [128, D], BF16))
        cwb = E(nc.sbuf_tensor("cwb", [128, 8], F32))
        seyeb = E(nc.sbuf_tensor("seyeb", [128, 128], BF16))
        seyef = E(nc.sbuf_tensor("seyef", [128, 128], F32))
        sonescf = E(nc.sbuf_tensor("sonescf", [128, 1], F32))
        sonesrf = E(nc.sbuf_tensor("sonesrf", [1, 128], F32))
        sehs = E(nc.sbuf_tensor("sehs", [128, 8 * 128], BF16))
        sconst = E(nc.sbuf_tensor("sconst", [128, 4], F32))
        km2row = E(nc.sbuf_tensor("km2row", [1, 16], F32))
        srow8 = E(nc.sbuf_tensor("srow8", [1, 8], F32))
        sninvr = E(nc.sbuf_tensor("sninvr", [1, 8], F32))
        swrow = E(nc.sbuf_tensor("swrow", [1, 8], F32))
        norm32 = E(nc.sbuf_tensor("norm32", [32, 32], F32))
        normT = E(nc.sbuf_tensor("normT", [32, 32], F32))
        qrow = E(nc.sbuf_tensor("qrow", [1, 16], F32))
        xn = E(nc.sbuf_tensor("xn", [128, D], F32))
        ybuf = E(nc.sbuf_tensor("ybuf", [128, D], F32))
        rnd32 = E(nc.sbuf_tensor("rnd32", [128, D], F32))
        xqtm = E(nc.sbuf_tensor("xqtm", [128, D], BF16))
        bnst = E(nc.sbuf_tensor("bnst", [128, 24], F32))
        mv = E(nc.sbuf_tensor("mv", [128, 2], F32))
        tt8 = E(nc.sbuf_tensor("tt8", [128, 8], F32))
        alj = E(nc.sbuf_tensor("alj", [128, 12], F32))

        psA = E(nc.psum_tensor("psA", [128, 512], F32))
        psB = E(nc.psum_tensor("psB", [128, 512], F32))
        psC = E(nc.psum_tensor("psC", [128, 512], F32))
        psD = E(nc.psum_tensor("psD", [128, 512], F32))
        psE = E(nc.psum_tensor("psE", [128, 512], F32))
        psF = E(nc.psum_tensor("psF", [128, 512], F32))

        ld = Ctr(E(nc.semaphore("ld")))
        gd = Ctr(E(nc.semaphore("gd")))
        te = Ctr(E(nc.semaphore("te")))
        ve = Ctr(E(nc.semaphore("ve")))
        se = Ctr(E(nc.semaphore("se")))
        blk = E(nc.Block())

        def barrier():
            cs = [ld, gd, te, ve, se]
            vals = [c.n for c in cs]
            for emit, own in [(blk.sync, ld), (blk.tensor, te), (blk.vector, ve),
                              (blk.scalar, se), (blk.gpsimd, gd)]:
                def f(e, own=own, vals=tuple(vals)):
                    for c, v in zip(cs, vals):
                        if c is not own:
                            e.wait_ge(c.sem, v)
                emit(f)

        def emit_ln(src):
            @blk.vector
            def _(e):
                e.wait_ge(se.sem, se.n)
                for q in range(4):
                    ins = e.bn_stats(bnst[:, q * 6:q * 6 + 6],
                                     src[:, q * 512:q * 512 + 512])
                ve.inc(ins)
                e.wait_ge(ve.sem, ve.n)
                ve.inc(e.bn_aggr(mv[:], bnst[:]))

            @blk.scalar
            def _(e):
                e.wait_ge(ve.sem, ve.n)
                e.activation(tt8[:, 0:1], mv[:, 1:2], ACT.Ln, bias=sconst[:, 0:1])
                se.inc(e.activation(tt8[:, 0:1], tt8[:, 0:1], ACT.Exp, scale=-0.5))

            @blk.vector
            def _(e):
                e.wait_ge(se.sem, se.n)
                e.scalar_tensor_tensor(tt8[:, 1:2], mv[:, 0:1], -1.0,
                                       tt8[:, 0:1], ALU.mult, ALU.mult)
                ve.inc(e.tensor_tensor(tt8[:, 3:4], tt8[:, 0:1], tt8[:, 0:1],
                                       ALU.mult))
                e.wait_ge(ve.sem, ve.n)
                ve.inc(e.tensor_tensor(tt8[:, 3:4], mv[:, 1:2], tt8[:, 3:4],
                                       ALU.mult))

            @blk.scalar
            def _(e):
                e.wait_ge(ve.sem, ve.n)
                e.activation(xn[:], src[:], ACT.Identity, bias=tt8[:, 1:2],
                             scale=tt8[:, 0:1])
                e.activation(tt8[:, 2:3], tt8[:, 3:4], ACT.Ln, bias=sconst[:, 1:2])
                se.inc(e.activation(tt8[:, 2:3], tt8[:, 2:3], ACT.Exp, scale=-0.5))

        def emit_rms(src):
            @blk.vector
            def _(e):
                e.wait_ge(se.sem, se.n)
                for q in range(4):
                    ins = e.bn_stats(bnst[:, q * 6:q * 6 + 6],
                                     src[:, q * 512:q * 512 + 512])
                ve.inc(ins)
                e.wait_ge(ve.sem, ve.n)
                ve.inc(e.bn_aggr(mv[:], bnst[:]))
                e.wait_ge(ve.sem, ve.n)
                ve.inc(e.tensor_tensor(tt8[:, 3:4], mv[:, 0:1], mv[:, 0:1],
                                       ALU.mult))
                e.wait_ge(ve.sem, ve.n)
                ve.inc(e.tensor_tensor(tt8[:, 3:4], mv[:, 1:2], tt8[:, 3:4],
                                       ALU.add))

            @blk.scalar
            def _(e):
                e.wait_ge(ve.sem, ve.n)
                e.activation(tt8[:, 2:3], tt8[:, 3:4], ACT.Ln, bias=sconst[:, 1:2])
                se.inc(e.activation(tt8[:, 2:3], tt8[:, 2:3], ACT.Exp, scale=-0.5))

        def emit_quant(src, w, gain, use_rms, wcol, acol, extra_acols=()):
            yy = src if gain is None else ybuf

            @blk.vector
            def _(e):
                e.wait_ge(se.sem, se.n)
                e.wait_ge(te.sem, te.n)
                if gain is not None:
                    e.tensor_tensor(ybuf[:, 0:w], src[:, 0:w], gain[:, 0:w],
                                    ALU.mult)
                ve.inc(e.reduce_max(tt8[:, 4:5], yy[:, 0:w], AX.X,
                                    apply_absolute_value=True))
                e.wait_ge(ve.sem, ve.n)
                if use_rms:
                    ve.inc(e.tensor_tensor(tt8[:, 4:5], tt8[:, 4:5],
                                           tt8[:, 2:3], ALU.mult))
                    e.wait_ge(ve.sem, ve.n)
                ve.inc(e.tensor_scalar(tt8[:, 4:5], tt8[:, 4:5], 1e-5, None,
                                       ALU.max))

            @blk.scalar
            def _(e):
                e.wait_ge(ve.sem, ve.n)
                e.activation(tt8[:, 5:6], tt8[:, 4:5], ACT.Ln)
                se.inc(e.activation(tt8[:, 5:6], tt8[:, 5:6], ACT.Exp,
                                    scale=-1.0))

            @blk.vector
            def _(e):
                e.wait_ge(se.sem, se.n)
                if use_rms:
                    ve.inc(e.tensor_tensor(tt8[:, 6:7], tt8[:, 5:6],
                                           tt8[:, 2:3], ALU.mult))
                else:
                    ve.inc(e.tensor_copy(tt8[:, 6:7], tt8[:, 5:6]))
                e.wait_ge(ve.sem, ve.n)
                nch = (w + D - 1) // D
                for q in range(nch):
                    lo, hi = q * D, min(w, (q + 1) * D)
                    e.tensor_scalar(rnd32[:, 0:hi - lo].bitcast(I32),
                                    yy[:, lo:hi], tt8[:, 6:7], 127.0,
                                    ALU.mult, ALU.mult)
                    e.tensor_copy(xqtm[:, lo:hi], rnd32[:, 0:hi - lo].bitcast(I32))
                last = e.tensor_tensor(alj[:, acol:acol + 1], tt8[:, 4:5],
                                       cwb[:, wcol:wcol + 1], ALU.mult)
                for (wc2, ac2) in extra_acols:
                    last = e.tensor_tensor(alj[:, ac2:ac2 + 1], tt8[:, 4:5],
                                           cwb[:, wc2:wc2 + 1], ALU.mult)
                ve.inc(last)

        def emit_transposes(tbs, src_tm, ncc, dstap_fn):
            nb = (ncc + 7) // 8
            for b in range(nb):
                hi = min(8, ncc - b * 8)

                @blk.tensor
                def _(e, b=b, hi=hi):
                    e.wait_ge(ve.sem, ve.n)
                    for q in range(hi):
                        cc = b * 8 + q
                        te.inc(e.transpose(
                            tbs[b % 2][:, q * 128:q * 128 + 128],
                            src_tm[:, cc * 128:cc * 128 + 128], seyeb[:]))

                @blk.vector
                def _(e, b=b, hi=hi):
                    e.wait_ge(te.sem, te.n)
                    ve.inc(e.tensor_copy(dstap_fn(b, hi),
                                         tbs[b % 2][:, 0:hi * 128]))

        def fm4(buf):
            return buf[:].rearrange("p (cc j f) -> p cc j f", j=4, f=128)

        def fm512(buf):
            return buf[:].rearrange("p (cc r) -> p cc r", r=512)

        with ExitStack() as pctx:
            pE = pctx.enter_context
            fbuf = [pE(nc.sbuf_tensor(f"fbuf{i}", [128, 2048], F32))
                    for i in range(4)]
            tbuf = [pE(nc.sbuf_tensor(f"tbuf{i}", [128, 2048], BF16))
                    for i in range(4)]
            st1p = [pE(nc.sbuf_tensor(f"st1p{i}", [128, 2048], F32))
                    for i in range(2)]
            scols = pE(nc.sbuf_tensor("scols", [128, 64], F32))
            sbcw = pE(nc.sbuf_tensor("sbcw", [128, 1], F32))
            rndp = pE(nc.sbuf_tensor("rndp", [128, 2048], F32))
            pst = Ctr(pE(nc.semaphore("pst")))

            @blk.sync
            def _(e):
                for dst, src in [(sonescf, onescf), (sonesrf, onesrf),
                                 (sninvr, ninvr)]:
                    ld.inc(e.dma_start(out=dst[:], in_=src), 16)
            ld_c0 = ld.n

            @blk.vector
            def _(e):
                ve.inc(e.memset(swrow[:], 0.0))

            WLIST = [(fwq, twq, 16, 2048, 0), (fwk, twk, 16, 2048, 1),
                     (fwv, twv, 4, 8192, 2), (fwo, two, 4, 8192, 3),
                     (fwg, twg, 13, 8192, 4), (fwu, twu, 13, 8192, 5),
                     (fwd, twd, 52, 2048, 6)]
            for (fsrc, tdst, ns, wid, widx) in WLIST:
                panels = [(s, c0) for s in range(ns)
                          for c0 in range(0, wid, 2048)]
                np_ = len(panels)
                ld0, gd0, ve0 = ld.n, gd.n, ve.n

                def emit_sum_loads(eng, par, ld0=ld0, gd0=gd0, ve0=ve0,
                                   fsrc=fsrc, panels=panels):
                    def f(e):
                        for i, (s, c0) in enumerate(panels):
                            if i % 2 != par:
                                continue
                            if i < 4:
                                e.wait_ge(ve.sem, ve0)
                            else:
                                e.wait_ge(ve.sem, ve0 + i - 3)
                            (ld if par == 0 else gd).inc(
                                e.dma_start(out=fbuf[i % 4][:],
                                            in_=fsrc[s][:, c0:c0 + 2048]), 16)
                    eng(f)
                emit_sum_loads(blk.sync, 0)
                emit_sum_loads(blk.gpsimd, 1)

                @blk.vector
                def _(e, ld0=ld0, gd0=gd0, np_=np_):
                    e.wait_ge(ld.sem, ld_c0)
                    for i in range(np_):
                        if i % 2 == 0:
                            e.wait_ge(ld.sem, ld0 + 16 * (i // 2 + 1))
                        else:
                            e.wait_ge(gd.sem, gd0 + 16 * (i // 2 + 1))
                        ve.inc(e.reduce_sum(scols[:, i:i + 1], fbuf[i % 4][:],
                                            AX.X, apply_absolute_value=True))
                    e.wait_ge(ve.sem, ve.n)
                    ve.inc(e.reduce_sum(tt8[:, 0:1], scols[:, 0:np_], AX.X))

                @blk.tensor
                def _(e, widx=widx):
                    e.wait_ge(ve.sem, ve.n)
                    te.inc(e.matmul(psE[0:1, widx:widx + 1], tt8[:, 0:1],
                                    sonescf[:], start=True, stop=True))

                @blk.scalar
                def _(e, widx=widx):
                    e.wait_ge(te.sem, te.n)
                    se.inc(e.copy(swrow[0:1, widx:widx + 1],
                                  psE[0:1, widx:widx + 1]))

                @blk.vector
                def _(e, widx=widx):
                    e.wait_ge(se.sem, se.n)
                    ve.inc(e.tensor_tensor(tt8[0:1, 1:2],
                                           swrow[0:1, widx:widx + 1],
                                           sninvr[0:1, widx:widx + 1],
                                           ALU.mult))
                    e.wait_ge(ve.sem, ve.n)
                    ve.inc(e.tensor_scalar(tt8[0:1, 1:2], tt8[0:1, 1:2], 1e-5,
                                           None, ALU.max))

                @blk.scalar
                def _(e):
                    e.wait_ge(ve.sem, ve.n)
                    e.activation(tt8[0:1, 1:2], tt8[0:1, 1:2], ACT.Ln)
                    se.inc(e.activation(tt8[0:1, 1:2], tt8[0:1, 1:2], ACT.Exp,
                                        scale=-1.0))

                @blk.tensor
                def _(e):
                    e.wait_ge(se.sem, se.n)
                    te.inc(e.matmul(psD[:, 0:1], sonesrf[:], tt8[0:1, 1:2],
                                    start=True, stop=True))

                @blk.vector
                def _(e):
                    e.wait_ge(te.sem, te.n)
                    ve.inc(e.tensor_copy(sbcw[:], psD[:, 0:1]))

                ld1, gd1, ve1, se1, pst0 = ld.n, gd.n, ve.n, se.n, pst.n

                def emit_tern_loads(eng, par, ld1=ld1, gd1=gd1, ve1=ve1,
                                    fsrc=fsrc, panels=panels):
                    def f(e):
                        for i, (s, c0) in enumerate(panels):
                            if i % 2 != par:
                                continue
                            if i < 4:
                                e.wait_ge(ve.sem, ve1)
                            else:
                                e.wait_ge(ve.sem, ve1 + i - 3)
                            (ld if par == 0 else gd).inc(
                                e.dma_start(out=fbuf[i % 4][:],
                                            in_=fsrc[s][:, c0:c0 + 2048]), 16)
                    eng(f)
                emit_tern_loads(blk.sync, 0)
                emit_tern_loads(blk.gpsimd, 1)

                @blk.vector
                def _(e, panels=panels, ld1=ld1, gd1=gd1, ve1=ve1, pst0=pst0):
                    e.wait_ge(ve.sem, ve1)
                    for i in range(len(panels)):
                        if i % 2 == 0:
                            e.wait_ge(ld.sem, ld1 + 16 * (i // 2 + 1))
                        else:
                            e.wait_ge(gd.sem, gd1 + 16 * (i // 2 + 1))
                        e.tensor_scalar(st1p[0][:], fbuf[i % 4][:],
                                        sbcw[:, 0:1], CLIP, ALU.mult, ALU.min)
                        e.tensor_scalar(rndp[:].bitcast(I32), st1p[0][:],
                                        -CLIP, None, ALU.max)
                        if i >= 4:
                            e.wait_ge(pst.sem, pst0 + 16 * (i - 3))
                        ve.inc(e.tensor_copy(tbuf[i % 4][:],
                                             rndp[:].bitcast(I32)))

                @blk.scalar
                def _(e, tdst=tdst, panels=panels, ve1=ve1):
                    for i, (s, c0) in enumerate(panels):
                        e.wait_ge(ve.sem, ve1 + i + 1)
                        pst.inc(e.dma_start(out=tdst[s][:, c0:c0 + 2048],
                                            in_=tbuf[i % 4][:]), 16)

            @blk.vector
            def _(e):
                e.wait_ge(pst.sem, pst.n)
                ve.inc(e.engine_nop())

            barrier()

        @blk.vector
        def _(e):
            e.memset(sconst[:, 0:1], 1e-5)
            e.memset(sconst[:, 1:2], 1e-6)
            e.memset(sconst[:, 2:3], 1e-30)
            e.memset(sconst[:, 3:4], 0.0)
            e.memset(norm32[:], 0.0)
            ve.inc(e.memset(alj[:], 0.0))

        @blk.sync
        def _(e):
            for dst, src in [(seyeb, eyeb), (seyef, eyef), (sehs, ehs8)]:
                ld.inc(e.dma_start(out=dst[:], in_=src), 16)

        @blk.vector
        def _(e):
            ve.inc(e.tensor_tensor(srow8[:], swrow[:], sninvr[:],
                                   ALU.mult))
            e.wait_ge(ve.sem, ve.n)
            ve.inc(e.tensor_scalar(srow8[:], srow8[:], 1e-5, None, ALU.max))
            e.wait_ge(ve.sem, ve.n)
            ve.inc(e.tensor_scalar(srow8[:], srow8[:], 1.0 / 127.0, None,
                                   ALU.mult))

        @blk.tensor
        def _(e):
            e.wait_ge(ve.sem, ve.n)
            te.inc(e.matmul(psF[:, 0:8], sonesrf[:], srow8[:],
                            start=True, stop=True))

        @blk.vector
        def _(e):
            e.wait_ge(te.sem, te.n)
            ve.inc(e.tensor_copy(cwb[:], psF[:, 0:8]))
            e.wait_ge(ve.sem, ve.n)
            ve.inc(e.tensor_scalar(cwb[:, 0:1], cwb[:, 0:1], RSQDK, None,
                                   ALU.mult))

        barrier()

        with ExitStack() as actx:
            aE = actx.enter_context
            gb = {"o": gbo}
            for p in "qkv":
                gb[p] = aE(nc.sbuf_tensor("gb" + p, [128, D], BF16))
            xbuf = [aE(nc.sbuf_tensor(f"xbuf{i}", [128, D], F32)) for i in range(2)]
            xqfm = {p: aE(nc.sbuf_tensor("xqfm" + p, [128, NCD * 512], BF16))
                    for p in "qkv"}
            wsl = [aE(nc.sbuf_tensor(f"wsl{i}", [128, NCD * 128], BF16))
                   for i in range(2)]
            vsl = [aE(nc.sbuf_tensor(f"vsl{i}", [128, NCD * 512], BF16))
                   for i in range(2)]
            qob = [aE(nc.sbuf_tensor(f"qob{i}", [128, 512], BF16)) for i in range(2)]
            vob = [aE(nc.sbuf_tensor(f"vob{i}", [128, 512], BF16)) for i in range(2)]
            abc = aE(nc.sbuf_tensor("abc", [128, 512], F32))
            arow = aE(nc.sbuf_tensor("arow", [1, 512], F32))
            tb0 = aE(nc.psum_tensor("tbA0", [128, 1024], BF16))
            tb1 = aE(nc.psum_tensor("tbA1", [128, 1024], BF16))
            tbs = [tb0, tb1]

            ld_g0 = ld.n
            ve_g0 = ve.n

            @blk.sync
            def _(e):
                for i, p in enumerate("qkvo"):
                    e.wait_ge(ve.sem, ve_g0 + i)
                    ld.inc(e.dma_start(
                        out=xbuf[0][:],
                        in_=g_in[p][None, :].partition_broadcast(128)), 16)

            @blk.vector
            def _(e):
                for i, p in enumerate("qkvo"):
                    e.wait_ge(ld.sem, ld_g0 + (i + 1) * 16)
                    ve.inc(e.tensor_copy(gb[p][:], xbuf[0][:]))

            barrier()

            for g in range(NGA):
                own = g < NGO
                projs = "qkv" if own else "kv"
                for j in range(4):
                    tt = g * 4 + j

                    @blk.sync
                    def _(e, j=j, tt=tt):
                        e.wait_ge(se.sem, se.n)
                        ld.inc(e.dma_start(out=xbuf[j % 2][:],
                                           in_=x_in[tt * 128:(tt + 1) * 128, :]),
                               16)

                    @blk.vector
                    def _(e):
                        e.wait_ge(ld.sem, ld.n)
                        ve.inc(e.engine_nop())

                    emit_ln(xbuf[j % 2])
                    for p in projs:
                        pi = "qkv".index(p)
                        emit_quant(xn, D, gb[p], True, "qkvo".index(p),
                                   pi * 4 + j)
                        emit_transposes(
                            tbs, xqtm, NCD,
                            lambda b, hi, p=p, j=j: fm4(xqfm[p])
                                [:, b * 8:b * 8 + hi, j, :])

                for p in ("qk" if own else "k"):
                    pi = "qkv".index(p)

                    @blk.tensor
                    def _(e, pi=pi):
                        e.wait_ge(ve.sem, ve.n)
                        for j in range(4):
                            te.inc(e.transpose(psF[0:1, j * 128:j * 128 + 128],
                                               alj[:, pi * 4 + j:pi * 4 + j + 1],
                                               seyef[:]))

                    @blk.vector
                    def _(e):
                        e.wait_ge(te.sem, te.n)
                        ve.inc(e.tensor_copy(arow[:], psF[0:1, 0:512]))

                    @blk.tensor
                    def _(e):
                        e.wait_ge(ve.sem, ve.n)
                        te.inc(e.matmul(psF[:, 0:512], sonesrf[:], arow[:],
                                        start=True, stop=True))

                    @blk.vector
                    def _(e):
                        e.wait_ge(te.sem, te.n)
                        ve.inc(e.tensor_copy(abc[:], psF[:, 0:512]))

                    tw = twq if p == "q" else twk
                    ld0, te0, ve0, gd0 = ld.n, te.n, ve.n, gd.n
                    ps = [psA, psB]

                    @blk.sync
                    def _(e, tw=tw, ld0=ld0, te0=te0):
                        for s in range(16):
                            e.wait_ge(te.sem, te0 + (s - 1) if s >= 2 else te0)
                            ld.inc(e.dma_start(out=wsl[s % 2][:], in_=tw[s]), 16)

                    @blk.tensor
                    def _(e, p=p, ld0=ld0, ve0=ve0):
                        for s in range(16):
                            e.wait_ge(ld.sem, ld0 + 16 * (s + 1))
                            e.wait_ge(ve.sem, ve0 + (s - 1) if s >= 2 else ve0)
                            for cc in range(NCD):
                                mm = e.matmul(
                                    ps[s % 2][:, 0:512],
                                    wsl[s % 2][:, cc * 128:cc * 128 + 128],
                                    fm512(xqfm[p])[:, cc, :],
                                    start=(cc == 0), stop=(cc == NCD - 1))
                            te.inc(mm)

                    @blk.vector
                    def _(e, te0=te0, gd0=gd0):
                        for s in range(16):
                            e.wait_ge(te.sem, te0 + s + 1)
                            e.wait_ge(gd.sem, gd0 + 16 * (s - 1) if s >= 2 else gd0)
                            ve.inc(e.tensor_tensor(qob[s % 2][:],
                                                   ps[s % 2][:, 0:512],
                                                   abc[:], ALU.mult))

                    @blk.gpsimd
                    def _(e, p=p, g=g, ve0=ve0):
                        for s in range(16):
                            e.wait_ge(ve.sem, ve0 + s + 1)
                            if p == "q":
                                dst = qscr[g, s]
                            else:
                                dst = kscr[s][:, g * 512:(g + 1) * 512]
                            gd.inc(e.dma_start(out=dst, in_=qob[s % 2][:]), 16)

                ld0, te0, se0, gd0 = ld.n, te.n, se.n, gd.n
                psv = [psC, psD]

                @blk.sync
                def _(e, ld0=ld0, te0=te0):
                    for ob in range(4):
                        if ob >= 2:
                            e.wait_ge(te.sem, te0 + (ob - 1) * 4)
                        ld.inc(e.dma_start(out=vsl[ob % 2][:], in_=twv[ob]), 16)

                @blk.tensor
                def _(e, ld0=ld0, se0=se0):
                    for ob in range(4):
                        e.wait_ge(ld.sem, ld0 + 16 * (ob + 1))
                        for j in range(4):
                            m = ob * 4 + j
                            if m >= 2:
                                e.wait_ge(se.sem, se0 + (m - 1))
                            for cc in range(NCD):
                                mm = e.matmul(
                                    psv[m % 2][:, 0:512],
                                    fm4(xqfm["v"])[:, cc, j, :],
                                    fm512(vsl[ob % 2])[:, cc, :],
                                    start=(cc == 0), stop=(cc == NCD - 1))
                            te.inc(mm)

                @blk.scalar
                def _(e, te0=te0, gd0=gd0):
                    for ob in range(4):
                        for j in range(4):
                            m = ob * 4 + j
                            e.wait_ge(te.sem, te0 + m + 1)
                            if m >= 2:
                                e.wait_ge(gd.sem, gd0 + 64 * (m - 1))
                            se.inc(e.activation(vob[m % 2][:],
                                                psv[m % 2][:, 0:512], ACT.Copy,
                                                scale=alj[:, 8 + j:8 + j + 1]))

                @blk.gpsimd
                def _(e, g=g, se0=se0):
                    for ob in range(4):
                        for j in range(4):
                            m = ob * 4 + j
                            tt = g * 4 + j
                            e.wait_ge(se.sem, se0 + m + 1)
                            for q in range(4):
                                gd.inc(e.dma_start(
                                    out=vscr[4 * ob + q][:, tt, :],
                                    in_=vob[m % 2][:, q * 128:q * 128 + 128]), 16)

                barrier()

        barrier()

        with ExitStack() as bctx:
            bE = bctx.enter_context
            kb0 = bE(nc.sbuf_tensor("kb0", [128, TB], BF16))
            k2 = bE(nc.sbuf_tensor("k2", [128, TB], BF16))
            for h in range(H):
                @blk.sync
                def _(e, h=h):
                    e.wait_ge(ve.sem, ve.n)
                    ld.inc(e.dma_start(out=kb0[:], in_=kscr[h]), 16)

                @blk.vector
                def _(e):
                    e.wait_ge(ld.sem, ld.n)
                    e.wait_ge(te.sem, te.n)
                    ve.inc(e.tensor_tensor(k2[:], kb0[:], kb0[:], ALU.mult))

                @blk.tensor
                def _(e):
                    e.wait_ge(ve.sem, ve.n)
                    for jj in range(8):
                        mm = e.matmul(psA[:, 0:512],
                                      sehs[:, jj * 128:jj * 128 + 128],
                                      k2[:, jj * 512:jj * 512 + 512],
                                      start=(jj == 0), stop=(jj == 7))
                    te.inc(mm)

                @blk.vector
                def _(e, h=h):
                    e.wait_ge(te.sem, te.n)
                    ve.inc(e.reduce_max(norm32[0:8, 0:1], psA[0:8, 0:512],
                                        AX.X))
                    e.wait_ge(ve.sem, ve.n)
                    ve.inc(e.transpose(normT[:], norm32[:]))
                    e.wait_ge(ve.sem, ve.n)
                    ve.inc(e.reduce_max(km2row[0:1, h:h + 1], normT[0:1, 0:8],
                                        AX.X))

        barrier()

        for g in range(NGO):
            with ExitStack() as gctx:
                gE = gctx.enter_context
                xqpfm = gE(nc.sbuf_tensor(f"xqpfm{g}", [128, NCD * 512], BF16))
                sC = ExitStack()
                v1 = [sC.enter_context(nc.sbuf_tensor(f"v1_{g}_{j}", [128, D], F32))
                      for j in range(4)]
                ao = [sC.enter_context(nc.sbuf_tensor(f"ao{g}{j}", [128, D], F32))
                      for j in range(4)]
                sB = ExitStack()
                attnout = sB.enter_context(
                    nc.sbuf_tensor(f"attnout{g}", [128, H * 512], BF16))
                negbc = sB.enter_context(nc.sbuf_tensor(f"negbc{g}", [128, 16], F32))
                negrow = sB.enter_context(nc.sbuf_tensor(f"negrow{g}", [1, 16], F32))
                ssbA = sB.enter_context(nc.sbuf_tensor(f"ssbA{g}", [8, 512], F32))
                ssbB = sB.enter_context(nc.sbuf_tensor(f"ssbB{g}", [8, 512], F32))
                rec = sB.enter_context(nc.sbuf_tensor(f"rec{g}", [128, 64], F32))
                sA = ExitStack()
                qgrp = sA.enter_context(
                    nc.sbuf_tensor(f"qgrp{g}", [128, H * 512], BF16))
                kb = [sA.enter_context(nc.sbuf_tensor(f"kb{g}{i}", [128, TB], BF16))
                      for i in range(2)]
                vb = [sA.enter_context(nc.sbuf_tensor(f"vb{g}{i}", [128, TB], BF16))
                      for i in range(2)]
                pbuf = sA.enter_context(
                    nc.sbuf_tensor(f"pbuf{g}", [128, 1024], BF16))

                @blk.sync
                def _(e, g=g):
                    e.wait_ge(gd.sem, gd.n)
                    for j in range(4):
                        tt = g * 4 + j
                        ld.inc(e.dma_start(
                            out=v1[j][:], in_=x_in[tt * 128:(tt + 1) * 128, :]), 16)
                    ld.inc(e.dma_start(
                        out=qgrp[:].rearrange("p (h f) -> p h f", f=512),
                        in_=qscr[g].rearrange("h p f -> p h f")), 16)

                @blk.vector
                def _(e):
                    e.wait_ge(ld.sem, ld.n)
                    ve.inc(e.tensor_tensor(attnout[:], qgrp[:], qgrp[:],
                                           ALU.mult))

                @blk.tensor
                def _(e):
                    e.wait_ge(ve.sem, ve.n)
                    for h in range(H):
                        bank = psA if h < 8 else psB
                        mm = e.matmul(bank[:, 0:512],
                                      sehs[:, (h % 8) * 128:(h % 8) * 128 + 128],
                                      attnout[:, h * 512:h * 512 + 512],
                                      start=(h % 8 == 0), stop=(h % 8 == 7))
                        if h % 8 == 7:
                            te.inc(mm)

                @blk.vector
                def _(e):
                    e.wait_ge(te.sem, te.n)
                    ve.inc(e.reduce_max(norm32[0:8, 0:1], psA[0:8, 0:512],
                                        AX.X))
                    e.wait_ge(ve.sem, ve.n)
                    ve.inc(e.transpose(normT[:], norm32[:]))
                    e.wait_ge(ve.sem, ve.n)
                    e.tensor_copy(qrow[0:1, 0:8], normT[0:1, 0:8])
                    ve.inc(e.reduce_max(norm32[0:8, 0:1], psB[0:8, 0:512],
                                        AX.X))
                    e.wait_ge(ve.sem, ve.n)
                    ve.inc(e.transpose(normT[:], norm32[:]))
                    e.wait_ge(ve.sem, ve.n)
                    ve.inc(e.tensor_copy(qrow[0:1, 8:16], normT[0:1, 0:8]))
                    e.wait_ge(ve.sem, ve.n)
                    ve.inc(e.tensor_tensor(negrow[:], qrow[:],
                                           km2row[:], ALU.mult))

                @blk.scalar
                def _(e):
                    e.wait_ge(ve.sem, ve.n)
                    e.activation(negrow[:], negrow[:], ACT.Ln,
                                 bias=sconst[0:1, 2:3])
                    se.inc(e.activation(negrow[:], negrow[:], ACT.Exp, scale=0.5))

                @blk.vector
                def _(e):
                    e.wait_ge(se.sem, se.n)
                    ve.inc(e.tensor_scalar(negrow[:], negrow[:], -1.001, None,
                                           ALU.mult))

                @blk.tensor
                def _(e):
                    e.wait_ge(ve.sem, ve.n)
                    te.inc(e.matmul(psF[:, 0:16], sonesrf[:], negrow[:],
                                    start=True, stop=True))

                @blk.vector
                def _(e):
                    e.wait_ge(te.sem, te.n)
                    ve.inc(e.tensor_copy(negbc[:], psF[:, 0:16]))

                barrier()

                ld0, te0, se0, ve0 = ld.n, te.n, se.n, ve.n
                teh = lambda h: te0 + 64 * h

                @blk.sync
                def _(e, ld0=ld0, te0=te0):
                    for h in range(H):
                        if h >= 2:
                            e.wait_ge(te.sem, teh(h - 1))
                        ld.inc(e.dma_start(out=kb[h % 2][:], in_=kscr[h]), 16)
                        ld.inc(e.dma_start(
                            out=vb[h % 2][:],
                            in_=vscr[h].rearrange("p t f -> p (t f)")), 16)

                @blk.tensor
                def _(e, ld0=ld0, se0=se0, ve0=ve0):
                    for h in range(H):
                        e.wait_ge(ld.sem, ld0 + 32 * (h + 1))
                        e.wait_ge(se.sem, se0 + 32 * h)
                        e.wait_ge(ve.sem, ve0 + h)
                        for kc in range(2):
                            te.inc(e.matmul(
                                [psA, psB][kc][:, 0:512],
                                kb[h % 2][:, kc * 128:kc * 128 + 128],
                                qgrp[:, h * 512:h * 512 + 512],
                                start=True, stop=True))
                        for kc in range(32):
                            e.wait_ge(se.sem, se0 + 32 * h + kc + 1)
                            e.matmul(psC[:, 0:512],
                                     vb[h % 2][:, kc * 128:kc * 128 + 128],
                                     pbuf[:, (kc % 2) * 512:(kc % 2) * 512 + 512],
                                     start=(kc == 0), stop=(kc == 31))
                            bank = psD if h < 8 else psE
                            te.inc(e.matmul(
                                bank[:, 0:512],
                                sehs[:, (h % 8) * 128:(h % 8) * 128 + 128],
                                pbuf[:, (kc % 2) * 512:(kc % 2) * 512 + 512],
                                start=(h % 8 == 0 and kc == 0),
                                stop=(h % 8 == 7 and kc == 31)))
                            if kc + 2 < 32:
                                te.inc(e.matmul(
                                    [psA, psB][kc % 2][:, 0:512],
                                    kb[h % 2][:, (kc + 2) * 128:(kc + 2) * 128 + 128],
                                    qgrp[:, h * 512:h * 512 + 512],
                                    start=True, stop=True))

                @blk.scalar
                def _(e, te0=te0):
                    for h in range(H):
                        for kc in range(32):
                            if kc < 2:
                                tev = teh(h) + kc + 1
                            else:
                                tev = teh(h) + 2 * kc
                            e.wait_ge(te.sem, tev)
                            se.inc(e.activation(
                                pbuf[:, (kc % 2) * 512:(kc % 2) * 512 + 512],
                                [psA, psB][kc % 2][:, 0:512], ACT.Exp,
                                bias=negbc[:, h:h + 1]))

                @blk.vector
                def _(e, te0=te0):
                    for h in range(H):
                        e.wait_ge(te.sem, teh(h + 1))
                        ve.inc(e.tensor_copy(attnout[:, h * 512:h * 512 + 512],
                                             psC[:, 0:512]))

                barrier()
                sA.close()

                @blk.vector
                def _(e):
                    e.tensor_copy(ssbA[:], psD[0:8, 0:512])
                    ve.inc(e.tensor_copy(ssbB[:], psE[0:8, 0:512]))

                @blk.tensor
                def _(e):
                    e.wait_ge(ve.sem, ve.n)
                    for j in range(4):
                        e.transpose(psF[:, j * 16:j * 16 + 8],
                                    ssbA[0:8, j * 128:j * 128 + 128],
                                    seyef[0:8, 0:8])
                        te.inc(e.transpose(psF[:, j * 16 + 8:j * 16 + 16],
                                           ssbB[0:8, j * 128:j * 128 + 128],
                                           seyef[0:8, 0:8]))

                @blk.scalar
                def _(e):
                    e.wait_ge(te.sem, te.n)
                    e.activation(rec[:], psF[:, 0:64], ACT.Ln)
                    se.inc(e.activation(rec[:], rec[:], ACT.Exp, scale=-1.0))

                with nc.psum_tensor(f"tbB0{g}", [128, 1024], BF16) as tg0, \
                     nc.psum_tensor(f"tbB1{g}", [128, 1024], BF16) as tg1:
                    tgs = [tg0, tg1]
                    for j in range(4):
                        for hb in range(2):
                            @blk.tensor
                            def _(e, j=j, hb=hb):
                                e.wait_ge(ve.sem, ve.n)
                                for q in range(8):
                                    h = hb * 8 + q
                                    te.inc(e.transpose(
                                        tgs[hb][:, q * 128:q * 128 + 128],
                                        attnout[:, h * 512 + j * 128:
                                                h * 512 + j * 128 + 128],
                                        seyeb[:]))

                            @blk.vector
                            def _(e, j=j, hb=hb):
                                e.wait_ge(te.sem, te.n)
                                e.wait_ge(se.sem, se.n)
                                ins = None
                                for q in range(8):
                                    h = hb * 8 + q
                                    ins = e.tensor_scalar(
                                        ao[j][:, h * 128:h * 128 + 128],
                                        tgs[hb][:, q * 128:q * 128 + 128],
                                        rec[:, j * 16 + h:j * 16 + h + 1],
                                        None, ALU.mult)
                                ve.inc(ins)

                    sB.close()
                    for j in range(4):
                        emit_rms(ao[j])
                        emit_quant(ao[j], D, gbo, True, 3, j)
                        emit_transposes(
                            tgs, xqtm, NCD,
                            lambda b, hi, j=j: fm4(xqpfm)
                                [:, b * 8:b * 8 + hi, j, :])

                barrier()

                sD = ExitStack()
                osl = [sD.enter_context(
                    nc.sbuf_tensor(f"osl{g}{i}", [128, NCD * 512], BF16))
                    for i in range(2)]
                ld0, te0, ve0 = ld.n, te.n, ve.n
                pso = [psA, psB]

                @blk.sync
                def _(e, ld0=ld0, te0=te0):
                    for ob in range(4):
                        if ob >= 2:
                            e.wait_ge(te.sem, te0 + (ob - 1) * 4)
                        ld.inc(e.dma_start(out=osl[ob % 2][:], in_=two[ob]), 16)

                @blk.tensor
                def _(e, ld0=ld0, ve0=ve0):
                    for ob in range(4):
                        e.wait_ge(ld.sem, ld0 + 16 * (ob + 1))
                        for j in range(4):
                            m = ob * 4 + j
                            if m >= 2:
                                e.wait_ge(ve.sem, ve0 + (m - 1))
                            for cc in range(NCD):
                                mm = e.matmul(
                                    pso[m % 2][:, 0:512],
                                    fm4(xqpfm)[:, cc, j, :],
                                    fm512(osl[ob % 2])[:, cc, :],
                                    start=(cc == 0), stop=(cc == NCD - 1))
                            te.inc(mm)

                @blk.vector
                def _(e, te0=te0):
                    for ob in range(4):
                        for j in range(4):
                            m = ob * 4 + j
                            e.wait_ge(te.sem, te0 + m + 1)
                            ve.inc(e.scalar_tensor_tensor(
                                v1[j][:, ob * 512:ob * 512 + 512],
                                pso[m % 2][:, 0:512], alj[:, j:j + 1],
                                v1[j][:, ob * 512:ob * 512 + 512],
                                ALU.mult, ALU.add))

                @blk.gpsimd
                def _(e, g=g, ve0=ve0):
                    for j in range(4):
                        e.wait_ge(ve.sem, ve0 + 13 + j)
                        row = (g * 4 + j) * 128
                        gd.inc(e.dma_start(out=x1scr[row:row + 128, :],
                                           in_=v1[j][:]), 16)

                barrier()
                sD.close()
                sC.close()

                sE = ExitStack()
                gsl = [sE.enter_context(
                    nc.sbuf_tensor(f"gsl{g}{i}", [128, NCD * 512], BF16))
                    for i in range(2)]
                usl = [sE.enter_context(
                    nc.sbuf_tensor(f"usl{g}{i}", [128, NCD * 512], BF16))
                    for i in range(2)]
                gu = [sE.enter_context(
                    nc.sbuf_tensor(f"gu{g}{j}", [128, IP], BF16))
                    for j in range(2)]
                gtmp = sE.enter_context(nc.sbuf_tensor(f"gtmp{g}", [128, 512], BF16))
                utmp = sE.enter_context(nc.sbuf_tensor(f"utmp{g}", [128, 512], BF16))
                guqfm = sE.enter_context(
                    nc.sbuf_tensor(f"guqfm{g}", [128, NCI * 256], BF16))
                dch = [sE.enter_context(
                    nc.sbuf_tensor(f"dch{g}{i}", [128, 2048], BF16))
                    for i in range(2)]
                otmp = [sE.enter_context(
                    nc.sbuf_tensor(f"otmp{g}{i}", [128, 512], F32))
                    for i in range(2)]
                x1b = [sE.enter_context(
                    nc.sbuf_tensor(f"x1b{g}{i}", [128, D], F32))
                    for i in range(2)]

                for pr in range(2):
                    with nc.psum_tensor(f"tbC0{g}{pr}", [128, 1024], BF16) as tc0, \
                         nc.psum_tensor(f"tbC1{g}{pr}", [128, 1024], BF16) as tc1:
                        tcs = [tc0, tc1]
                        for j2 in range(2):
                            t = pr * 2 + j2

                            @blk.sync
                            def _(e, g=g, t=t, j2=j2):
                                e.wait_ge(ve.sem, ve.n)
                                row = (g * 4 + t) * 128
                                ld.inc(e.dma_start(out=x1b[j2][:],
                                                   in_=x1scr[row:row + 128, :]),
                                       16)

                            @blk.vector
                            def _(e):
                                e.wait_ge(ld.sem, ld.n)
                                ve.inc(e.engine_nop())

                            emit_ln(x1b[j2])
                            emit_quant(xn, D, None, False, 4, j2,
                                       extra_acols=[(5, 4 + j2)])
                            emit_transposes(
                                tcs, xqtm, NCD,
                                lambda b, hi, j2=j2: fm4(xqpfm)
                                    [:, b * 8:b * 8 + hi, j2, :])

                        barrier()

                    ld0, te0, se0, ve0 = ld.n, te.n, se.n, ve.n
                    pg = [psA, psB]
                    pu = [psC, psD]

                    @blk.sync
                    def _(e, ld0=ld0, te0=te0):
                        for hb in range(13):
                            e.wait_ge(te.sem,
                                      te0 + (hb - 1) * 2 if hb >= 2 else te0)
                            ld.inc(e.dma_start(out=gsl[hb % 2][:], in_=twg[hb]),
                                   16)
                            ld.inc(e.dma_start(out=usl[hb % 2][:], in_=twu[hb]),
                                   16)

                    @blk.tensor
                    def _(e, ld0=ld0, se0=se0):
                        for hb in range(13):
                            e.wait_ge(ld.sem, ld0 + 32 * (hb + 1))
                            for j2 in range(2):
                                m = hb * 2 + j2
                                e.wait_ge(se.sem,
                                          se0 + (m - 1) if m >= 2 else se0)
                                for cc in range(NCD):
                                    xq = fm4(xqpfm)[:, cc, j2, :]
                                    e.matmul(
                                        pg[m % 2][:, 0:512], xq,
                                        fm512(gsl[hb % 2])[:, cc, :],
                                        start=(cc == 0), stop=(cc == NCD - 1))
                                    mm = e.matmul(
                                        pu[m % 2][:, 0:512], xq,
                                        fm512(usl[hb % 2])[:, cc, :],
                                        start=(cc == 0), stop=(cc == NCD - 1))
                                te.inc(mm)

                    @blk.scalar
                    def _(e, te0=te0, ve0=ve0):
                        for hb in range(13):
                            for j2 in range(2):
                                m = hb * 2 + j2
                                e.wait_ge(te.sem, te0 + m + 1)
                                if m >= 1:
                                    e.wait_ge(ve.sem, ve0 + m)
                                e.activation(gtmp[:], pg[m % 2][:, 0:512],
                                             ACT.Silu, scale=alj[:, j2:j2 + 1])
                                se.inc(e.activation(utmp[:], pu[m % 2][:, 0:512],
                                                    ACT.Copy,
                                                    scale=alj[:, 4 + j2:5 + j2]))

                    @blk.vector
                    def _(e, se0=se0):
                        for hb in range(13):
                            for j2 in range(2):
                                m = hb * 2 + j2
                                e.wait_ge(se.sem, se0 + m + 1)
                                ve.inc(e.tensor_tensor(
                                    gu[j2][:, hb * 512:hb * 512 + 512],
                                    gtmp[:], utmp[:], ALU.mult))

                    barrier()

                    with nc.psum_tensor(f"tbD0{g}{pr}", [128, 1024], BF16) as td0, \
                         nc.psum_tensor(f"tbD1{g}{pr}", [128, 1024], BF16) as td1:
                        tds = [td0, td1]
                        for j2 in range(2):
                            t = pr * 2 + j2

                            @blk.vector
                            def _(e, j2=j2):
                                e.wait_ge(se.sem, se.n)
                                e.wait_ge(te.sem, te.n)
                                ve.inc(e.reduce_max(
                                    tt8[:, 4:5], gu[j2][:], AX.X,
                                    apply_absolute_value=True))
                                e.wait_ge(ve.sem, ve.n)
                                ve.inc(e.tensor_scalar(tt8[:, 4:5],
                                                       tt8[:, 4:5], 1e-5,
                                                       None, ALU.max))

                            @blk.scalar
                            def _(e):
                                e.wait_ge(ve.sem, ve.n)
                                e.activation(tt8[:, 6:7], tt8[:, 4:5], ACT.Ln)
                                se.inc(e.activation(tt8[:, 6:7], tt8[:, 6:7],
                                                    ACT.Exp, scale=-1.0))

                            @blk.vector
                            def _(e, t=t, j2=j2):
                                e.wait_ge(se.sem, se.n)
                                for q in range(4):
                                    lo = q * 1664
                                    e.tensor_scalar(
                                        rnd32[:, 0:1664].bitcast(I32),
                                        gu[j2][:, lo:lo + 1664], tt8[:, 6:7],
                                        127.0, ALU.mult, ALU.mult)
                                    e.tensor_copy(gu[j2][:, lo:lo + 1664],
                                                  rnd32[:, 0:1664].bitcast(I32))
                                ve.inc(e.tensor_tensor(
                                    alj[:, 8 + t:8 + t + 1], tt8[:, 4:5],
                                    cwb[:, 6:7], ALU.mult))

                            emit_transposes(
                                tds, gu[j2], NCI,
                                lambda b, hi, j2=j2: guqfm[:].rearrange(
                                    "p (cc j f) -> p cc j f", j=2, f=128)
                                    [:, b * 8:b * 8 + hi, j2, :])

                        barrier()

                    with nc.psum_tensor(f"pdG{g}{pr}", [128, 512], F32) as psG, \
                         nc.psum_tensor(f"pdH{g}{pr}", [128, 512], F32) as psH:
                        pd = [psA, psB, psC, psD, psE, psF, psG, psH]
                        ld0, te0, ve0, gd0 = ld.n, te.n, ve.n, gd.n

                        @blk.sync
                        def _(e, ld0=ld0, te0=te0):
                            for ci in range(NCI):
                                if ci >= 2:
                                    e.wait_ge(te.sem, te0 + (ci - 1))
                                ld.inc(e.dma_start(out=dch[ci % 2][:],
                                                   in_=twd[ci]), 16)

                        @blk.tensor
                        def _(e, ld0=ld0):
                            for ci in range(NCI):
                                e.wait_ge(ld.sem, ld0 + 16 * (ci + 1))
                                for j2 in range(2):
                                    for ob in range(4):
                                        mm = e.matmul(
                                            pd[j2 * 4 + ob][:, 0:512],
                                            guqfm[:].rearrange(
                                                "p (cc j f) -> p cc j f",
                                                j=2, f=128)[:, ci, j2, :],
                                            dch[ci % 2][:, ob * 512:
                                                        ob * 512 + 512],
                                            start=(ci == 0),
                                            stop=(ci == NCI - 1))
                                te.inc(mm)

                        @blk.vector
                        def _(e, g=g, pr=pr, te0=te0, gd0=gd0):
                            e.wait_ge(te.sem, te0 + NCI)
                            for j2 in range(2):
                                t = pr * 2 + j2
                                for ob in range(4):
                                    m = j2 * 4 + ob
                                    if m >= 2:
                                        e.wait_ge(gd.sem, gd0 + 16 * (m - 1))
                                    ve.inc(e.scalar_tensor_tensor(
                                        otmp[m % 2][:],
                                        pd[m][:, 0:512],
                                        alj[:, 8 + t:8 + t + 1],
                                        x1b[j2][:, ob * 512:ob * 512 + 512],
                                        ALU.mult, ALU.add))

                        @blk.gpsimd
                        def _(e, g=g, pr=pr, ve0=ve0):
                            for j2 in range(2):
                                t = pr * 2 + j2
                                row = (g * 4 + t) * 128
                                for ob in range(4):
                                    m = j2 * 4 + ob
                                    e.wait_ge(ve.sem, ve0 + m + 1)
                                    gd.inc(e.dma_start(
                                        out=out[row:row + 128,
                                                ob * 512:ob * 512 + 512],
                                        in_=otmp[m % 2][:]), 16)

                    barrier()
                sE.close()

        @blk.gpsimd
        def _(e):
            e.wait_ge(gd.sem, gd.n)

    return nc



def kernel(L0_attn_x, L0_attn_wq, L0_attn_wk, L0_attn_wv, L0_attn_wo,
           L0_attn_gq, L0_attn_gk, L0_attn_gv, L0_attn_go,
           L0_ffn_w_gate, L0_ffn_w_up, L0_ffn_w_down):
    f32 = np.float32
    x = np.ascontiguousarray(np.asarray(L0_attn_x, f32))

    def lay_dd(w):
        a = np.asarray(w, f32).reshape(16, 128, 16, 128)
        return np.ascontiguousarray(a.transpose(0, 3, 2, 1)).reshape(
            16, 128, 2048)

    def lay_512(w):
        a = np.asarray(w, f32).reshape(4, 4, 128, 16, 128)
        return np.ascontiguousarray(a.transpose(0, 4, 3, 1, 2)).reshape(
            4, 128, NCD * 512)

    def lay_gu(w):
        wp = np.zeros((IP, D), f32)
        wp[0:IREAL] = np.asarray(w, f32)
        a = wp.reshape(13, 4, 128, 16, 128)
        return np.ascontiguousarray(a.transpose(0, 4, 3, 1, 2)).reshape(
            13, 128, NCD * 512)

    def lay_dn(w):
        wp = np.zeros((D, IP), f32)
        wp[:, 0:IREAL] = np.asarray(w, f32)
        a = wp.reshape(16, 128, 52, 128)
        return np.ascontiguousarray(a.transpose(2, 3, 0, 1)).reshape(
            52, 128, 2048)

    fwq_h = lay_dd(L0_attn_wq)
    fwk_h = lay_dd(L0_attn_wk)
    fwv_h = lay_512(L0_attn_wv)
    fwo_h = lay_512(L0_attn_wo)
    fwg_h = lay_gu(L0_ffn_w_gate)
    fwu_h = lay_gu(L0_ffn_w_up)
    fwd_h = lay_dn(L0_ffn_w_down)

    eyefn = np.eye(128, dtype=f32)
    eyebn = np.eye(128, dtype=f32).astype(BF16NP)
    onescf = np.ones((128, 1), f32)
    onesrf = np.ones((1, 128), f32)
    ehs = np.zeros((128, 8 * 128), f32)
    for jj in range(8):
        ehs[:, jj * 128 + jj] = 1.0
    ehs = ehs.astype(BF16NP)
    ninvr = np.array([1.0 / n for n in NUMEL] + [1.0], f32)[0:8].reshape(1, 8)

    nc = build_main()
    in_maps = []
    for r in range(8):
        b = r // 2
        xb = x[b]
        if r % 2 == 1:
            xb = np.concatenate([xb[2048:], xb[:2048]], axis=0)
        in_maps.append(dict(
            x=np.ascontiguousarray(xb), fwq=fwq_h, fwk=fwk_h, fwv=fwv_h,
            fwo=fwo_h, fwg=fwg_h, fwu=fwu_h, fwd=fwd_h,
            ninvr=ninvr,
            gq=np.asarray(L0_attn_gq, f32), gk=np.asarray(L0_attn_gk, f32),
            gv=np.asarray(L0_attn_gv, f32), go=np.asarray(L0_attn_go, f32),
            eyeb=eyebn, eyef=eyefn, onescf=onescf, onesrf=onesrf, ehs8=ehs))
    res = run_bass_kernel_spmd(nc, in_maps, core_ids=list(range(8)))

    outp = np.empty((4, TB, D), np.float32)
    for b in range(4):
        outp[b, 0:2048] = np.asarray(res.results[2 * b]["out"], np.float32)
        outp[b, 2048:] = np.asarray(res.results[2 * b + 1]["out"], np.float32)
    return outp
